# version 22
# baseline (speedup 1.0000x reference)
import numpy as np
from contextlib import ExitStack

import concourse.bass as bass
import concourse.mybir as mybir
from concourse.bass_utils import run_bass_kernel_spmd

F32 = mybir.dt.float32
BF16 = mybir.dt.bfloat16
FP16 = mybir.dt.float16
I32 = mybir.dt.int32
AF = mybir.ActivationFunctionType
ALU = mybir.AluOpType
AX = mybir.AxisListType

MAGIC = 12582912.0
D = 1024
NT = 64
NO = 16
HID = 4096
NW = 32
WMEAN = 1.0 / (HID * D)

NRED_PT = 2
T_SUM = 35
T_SSC = 36
T_BC = 37
T_ISB = 38
T_TERN = 39
NTERN_PT = 2



_DATA_OPS = {
    "tensor_scalar", "tensor_tensor", "tensor_copy", "tensor_add",
    "tensor_sub", "tensor_mul", "tensor_max", "tensor_single_scalar",
    "tensor_scalar_add", "tensor_scalar_sub", "tensor_scalar_mul",
    "tensor_scalar_max", "tensor_scalar_min", "scalar_tensor_tensor",
    "reduce_sum", "reduce_max", "tensor_reduce", "reciprocal", "memset",
    "affine_select", "activation", "copy", "add", "mul", "sign", "square",
    "bn_stats", "bn_aggr", "select",
}


class _Pending:
    def __init__(self, ser):
        self.ser = ser

    def then_inc(self, sem, amt=1):
        self.ser._milestone(sem, amt)
        return self


class Ser:

    def __init__(self, selfsem, counts):
        self.e = None
        self.s = selfsem
        self.n = 0
        self.counts = counts
        self.pending = None
        self.last = None

    def bind(self, eng):
        self.e = eng
        return self

    def _flush(self):
        if self.pending is not None:
            self.n += 1
            self.pending.then_inc(self.s, 1)
            self.last = (self.s, self.n)
            self.pending = None

    def _milestone(self, sem, amt):
        assert self.pending is not None
        v = self.counts.get(id(sem), 0) + amt
        self.counts[id(sem)] = v
        self.pending.then_inc(sem, amt)
        self.last = (sem, v)
        self.pending = None

    def wait_ge(self, sem, v):
        self.e.wait_ge(sem, v)

    def dma_start(self, *a, **k):
        self._flush()
        return self.e.dma_start(*a, **k)

    def end(self):
        self._flush()

    def __getattr__(self, name):
        f = getattr(self.e, name)
        if name in _DATA_OPS:
            def g(*a, **k):
                self._flush()
                if self.last is not None:
                    self.e.wait_ge(*self.last)
                self.pending = f(*a, **k)
                return _Pending(self)
            return g
        return f


def emit_rsqrt(v, out, var, eps, scr):
    x = scr[:, 0:1]
    t = scr[:, 1:2]
    v.tensor_scalar(x, var, float(eps), None, ALU.add)
    v.tensor_scalar(out.bitcast(I32), x.bitcast(I32), 1, None,
                    ALU.logical_shift_right)
    v.tensor_scalar(out.bitcast(I32), out.bitcast(I32), 0x5F3759DF, -1,
                    ALU.subtract, ALU.mult)
    for _ in range(2):
        v.tensor_mul(t, out, out)
        v.tensor_mul(t, t, x)
        v.tensor_scalar(t, t, -0.5, 1.5, ALU.mult, ALU.add)
        v.tensor_mul(out, out, t)


def emit_lntiny(v, stat, mu_c, var_c, rstd_c, bias_c, eps, sum_c=20, sq_c=21):
    v.tensor_scalar(stat[:, mu_c:mu_c + 1], stat[:, sum_c:sum_c + 1],
                    1.0 / 1024.0, None, ALU.mult)
    v.tensor_mul(stat[:, 22:23], stat[:, mu_c:mu_c + 1],
                 stat[:, mu_c:mu_c + 1])
    v.scalar_tensor_tensor(stat[:, var_c:var_c + 1], stat[:, sq_c:sq_c + 1],
                           1.0 / 1024.0, stat[:, 22:23], ALU.mult,
                           ALU.subtract)
    emit_rsqrt(v, stat[:, rstd_c:rstd_c + 1], stat[:, var_c:var_c + 1],
               eps, stat[:, 32:34])
    v.tensor_mul(stat[:, bias_c:bias_c + 1], stat[:, mu_c:mu_c + 1],
                 stat[:, rstd_c:rstd_c + 1])
    return v.tensor_scalar(stat[:, bias_c:bias_c + 1],
                           stat[:, bias_c:bias_c + 1], -1.0, None, ALU.mult)


def build_program():
    nc = bass.Bass("TRN2", target_bir_lowering=False)

    xTt = nc.dram_tensor("xTt", [NT * 128, D], F32, kind="ExternalInput").ap()
    x_own = nc.dram_tensor("x_own", [2048, D], F32, kind="ExternalInput").ap()
    w_qkv = nc.dram_tensor("w_qkv", [D, 1536], F32, kind="ExternalInput").ap()
    b_qkv = nc.dram_tensor("b_qkv", [1, 1536], F32, kind="ExternalInput").ap()
    bqv_in = nc.dram_tensor("bqv_in", [128, 4], F32, kind="ExternalInput").ap()
    w_fc = nc.dram_tensor("w_fc", [512, D], F32, kind="ExternalInput").ap()
    b_fc = nc.dram_tensor("b_fc", [1, D], F32, kind="ExternalInput").ap()
    gamma = nc.dram_tensor("gamma", [1, D], F32, kind="ExternalInput").ap()
    beta = nc.dram_tensor("beta", [1, D], F32, kind="ExternalInput").ap()
    w1 = nc.dram_tensor("w1", [HID, D], F32, kind="ExternalInput").ap()
    w2 = nc.dram_tensor("w2", [D, HID], F32, kind="ExternalInput").ap()
    y = nc.dram_tensor("y", [2048, D], F32, kind="ExternalOutput").ap()

    ctx = ExitStack()
    with ctx:
        E = ctx.enter_context
        w1T = E(nc.sbuf_tensor("w1T", [128, 8, HID], BF16))
        w2T = E(nc.sbuf_tensor("w2T", [128, 32, D], BF16))
        vt = E(nc.sbuf_tensor("vt", [128, 4, 2048], BF16))
        ident = E(nc.sbuf_tensor("ident", [128, 128], BF16))
        ones_bf = E(nc.sbuf_tensor("ones_bf", [1, 512], BF16))
        ones_col = E(nc.sbuf_tensor("ones_col", [128, 1], F32))
        ones_rowf = E(nc.sbuf_tensor("ones_rowf", [1, 128], F32))
        cmag = E(nc.sbuf_tensor("cmag", [128, 2], F32))
        bqkv_bf = E(nc.sbuf_tensor("bqkv_bf", [1, 1536], BF16))
        bqv = E(nc.sbuf_tensor("bqv", [128, 4], F32))
        attn_sb = E(nc.sbuf_tensor("attn_sb", [128, 256], BF16))
        smax = E(nc.sbuf_tensor("smax", [128, 16], F32))
        ssc = E(nc.sbuf_tensor("ssc", [1, 8], F32))
        isb = E(nc.sbuf_tensor("isb", [128, 6], F32))
        stat = E(nc.sbuf_tensor("stat", [128, 40], F32))

        sems = {}
        for name in ("GP", "VCON", "VSELF", "SSELF", "GSELF",
                     "LM", "LQ", "LO", "LWA", "LWB", "LS",
                     "TQ", "TS", "TVJ", "TWT", "TSM", "TGB",
                     "VQ", "VSA", "VWA", "VWB", "VWT", "VSC", "VSM1",
                     "SV", "SSM", "SWR", "SM",
                     "TAO", "TFC", "TXQ", "TH", "TGT", "TF2",
                     "VAO", "VR", "VS1", "VS2", "VQ1", "VX1", "VXQ",
                     "VQ2", "VGQ", "VGT", "VS3", "VGB", "VCC", "GRD",
                     "LQ0", "LQ1", "LQ2", "LQ3", "GAB",
                     "SY", "SX1", "SXQ", "SG", "SR", "SO", "OUT",
                     "VRB", "VRC", "SQ1", "SQ2", "SQ3"):
            sems[name] = E(nc.semaphore(name))
        S = type("S", (), sems)

        actx = ExitStack()
        Ea = actx.enter_context
        wqkv_sb = Ea(nc.sbuf_tensor("wqkv_sb", [128, 8, 1536], BF16))
        sc_acc = Ea(nc.sbuf_tensor("sc_acc", [128, 256], F32))
        ring = Ea(nc.sbuf_tensor("ring", [128, 4, 1024], BF16))
        qk_sb = Ea(nc.sbuf_tensor("qk_sb", [128, 2, 1024], BF16))
        wchk = Ea(nc.sbuf_tensor("wchk", [128, 2, 1024], BF16))
        wf32 = Ea(nc.sbuf_tensor("wf32", [128, 2, 1024], F32))
        sgb = Ea(nc.sbuf_tensor("sgb", [128, 1024], BF16))
        wpart = Ea(nc.sbuf_tensor("wpart", [128, 68], F32))

        pctx = ExitStack()
        Ep = pctx.enter_context
        qkps = Ep(nc.psum_tensor("qkps", [128, 2, 1024], F32))
        sps = Ep(nc.psum_tensor("sps", [128, 512], F32))
        vps = Ep(nc.psum_tensor("vps", [128, 2, 512], F32))
        wtps = Ep(nc.psum_tensor("wtps", [128, 1024], BF16))

        block = E(nc.Block())

        _counts = {}
        VW = Ser(sems["VSELF"], _counts)
        SW = Ser(sems["SSELF"], _counts)
        GW = Ser(sems["GSELF"], _counts)

        w1v = w1.rearrange("(r p) i -> p r i", p=128)
        w2v = w2.rearrange("(r p) (q i) -> p r q i", p=128, i=1024)

        def wsrc(c):
            if c < NW:
                return w1v[:, c, :]
            c2 = c - NW
            return w2v[:, c2 // 4, c2 % 4, :]

        def wsrc_pair(p):
            c = 2 * p
            if c < NW:
                return w1v[:, c:c + 2, :]
            c2 = c - NW
            return w2v[:, c2 // 4, (c2 % 4):(c2 % 4) + 2, :]

        @block.gpsimd
        def _(g):
            g = GW.bind(g)
            g.memset(ident[:], 0.0)
            g.affine_select(out=ident[:], in_=ident[:],
                            compare_op=ALU.not_equal, fill=1.0, base=0,
                            pattern=[[-1, 128]], channel_multiplier=1,
                            ).then_inc(S.GP, 1)
            g.dma_start(out=bqkv_bf[:, :], in_=b_qkv[:, :]).then_inc(S.LM, 16)
            g.wait_ge(S.LM, 16)
            g.dma_start(out=wqkv_sb[:, :, :],
                        in_=w_qkv.rearrange("(k p) c -> p k c", p=128)
                        ).then_inc(S.LM, 16)
            lqs = [S.LQ0, S.LQ1, S.LQ2, S.LQ3]
            nwa = 0
            for t in range(NT):
                if t >= 4:
                    tr = t - 4
                    g.wait_ge(S.TQ, tr + 1)
                    if tr < 16:
                        g.wait_ge(S.TVJ, 4 * (tr // 4 + 1))
                g.dma_start(
                    out=ring[:, t % 4, :],
                    in_=xTt[t * 128:(t + 1) * 128, :],
                ).then_inc(lqs[t % 4], 16)
                for _i in range(1):
                    if nwa < NW:
                        p = nwa
                        if p >= 1:
                            g.wait_ge(S.LWA, 16 * p)
                            g.wait_ge(S.VWA, 2 * p + (1 if p >= 16 else 0))
                        g.dma_start(out=wf32[:, :, :], in_=wsrc_pair(p)
                                    ).then_inc(S.LWA, 16)
                        nwa += 1


        @block.sync
        def _(s):
            s.dma_start(out=bqv[:, :], in_=bqv_in[:, :]).then_inc(S.LS, 16)
            for p in range(NW):
                if p == 0:
                    s.wait_ge(S.VWA, 66)
                if p >= 1:
                    s.wait_ge(S.LWB, 16 * p)
                    s.wait_ge(S.VWB, 2 * p)
                s.dma_start(out=wf32[:, :, :], in_=wsrc_pair(p)
                            ).then_inc(S.LWB, 16)

        @block.tensor
        def _(te):
            te.wait_ge(S.LM, 32)
            te.wait_ge(S.GP, 1)
            te.wait_ge(S.VCON, 1)
            nwt = 0

            def tern_transposes(c):
                te.wait_ge(S.VWB, c + 1)
                if c >= 1:
                    te.wait_ge(S.VWT, c)
                mm = None
                for kb in range(8):
                    mm = te.transpose(wtps[:, kb * 128:(kb + 1) * 128],
                                      wchk[:, c % 2, kb * 128:(kb + 1) * 128],
                                      ident[:])
                mm.then_inc(S.TWT, 1)

            lqs = [S.LQ0, S.LQ1, S.LQ2, S.LQ3]
            for t in range(NT):
                te.wait_ge(lqs[t % 4], 16 * (t // 4 + 1))
                if t >= 2:
                    te.wait_ge(S.VQ, t - 1)
                sl = t % 2
                mm = None
                for half in range(2):
                    for k in range(8):
                        te.matmul(qkps[:, sl, half * 512:(half + 1) * 512],
                                  ring[:, t % 4, k * 128:(k + 1) * 128],
                                  wqkv_sb[:, k, half * 512:(half + 1) * 512],
                                  start=(k == 0), stop=False)
                    mm = te.matmul(qkps[:, sl, half * 512:(half + 1) * 512],
                                   ones_bf[0:1, 0:128],
                                   bqkv_bf[0:1, half * 512:(half + 1) * 512],
                                   start=False, stop=True)
                mm.then_inc(S.TQ, 1)
                if t >= 1:
                    tp = t - 1
                    te.wait_ge(S.VQ, tp + 1)
                    if tp >= 1:
                        te.wait_ge(S.VSA, tp)
                    mm = None
                    for h in range(8):
                        pb = (h % 2) * 64
                        mm = te.matmul(
                            sps[pb:pb + 64, (h // 2) * 64:(h // 2) * 64 + 64],
                            qk_sb[:, tp % 2, h * 64:(h + 1) * 64],
                            qk_sb[:, tp % 2, 512 + h * 64:512 + (h + 1) * 64],
                            start=True, stop=True, tile_position=(0, pb))
                    mm.then_inc(S.TS, 1)
                if t < 16 and t % 4 == 3:
                    w = t // 4
                    for j in range(4):
                        if 4 * w + j >= 2:
                            te.wait_ge(S.SV, 4 * w + j - 1)
                        mm = None
                        for k in range(8):
                            mm = te.matmul(
                                vps[:, j % 2, :],
                                wqkv_sb[:, k, 1024 + j * 128:1024 + (j + 1) * 128],
                                ring[:, 0:4, k * 128:(k + 1) * 128],
                                start=(k == 0), stop=(k == 7))
                        mm.then_inc(S.TVJ, 1)
                if t == T_SUM:
                    te.wait_ge(S.GP, 1)
                    te.wait_ge(S.VWA, 66)
                    te.matmul(sps[0:1, 300:301], ones_col[:, 0:1],
                              wpart[:, 64:65], start=True, stop=True)
                    te.matmul(sps[0:1, 301:302], ones_col[:, 0:1],
                              wpart[:, 65:66], start=True, stop=True
                              ).then_inc(S.TSM, 1)
                if t == T_BC:
                    te.wait_ge(S.VSC, 1)
                    te.matmul(sps[:, 302:303], ones_rowf[0:1, :],
                              ssc[0:1, 0:1], start=True, stop=True)
                    te.matmul(sps[:, 303:304], ones_rowf[0:1, :],
                              ssc[0:1, 1:2], start=True, stop=True
                              ).then_inc(S.TSM, 1)
                if t >= T_TERN + 1:
                    for _i in range(NTERN_PT + 1):
                        if nwt < 2 * NW and nwt < (t - T_TERN) * NTERN_PT:
                            tern_transposes(nwt)
                            nwt += 1
            te.wait_ge(S.VQ, 64)
            te.wait_ge(S.VSA, 63)
            mm = None
            for h in range(8):
                pb = (h % 2) * 64
                mm = te.matmul(
                    sps[pb:pb + 64, (h // 2) * 64:(h // 2) * 64 + 64],
                    qk_sb[:, 1, h * 64:(h + 1) * 64],
                    qk_sb[:, 1, 512 + h * 64:512 + (h + 1) * 64],
                    start=True, stop=True, tile_position=(0, pb))
            mm.then_inc(S.TS, 1)
            while nwt < 2 * NW:
                tern_transposes(nwt)
                nwt += 1

        @block.vector
        def _(v):
            v = VW.bind(v)
            v.memset(ones_bf[:, :], 1.0)
            v.memset(ones_col[:, :], 1.0)
            v.memset(ones_rowf[:, :], 1.0)
            v.memset(cmag[:, 0:1], MAGIC)
            v.memset(cmag[:, 1:2], -MAGIC).then_inc(S.VCON, 1)
            nge = 0
            nev = 0
            nred = 0

            def tern_ge(c):
                v.wait_ge(S.VSC, 2)
                v.wait_ge(S.LWB, 16 * (c // 2 + 1))
                if c < 2:
                    v.wait_ge(S.VWA, 66)
                if c >= 2:
                    v.wait_ge(S.TWT, c - 1)
                hs = isb[:, 0:1] if c < NW else isb[:, 1:2]
                nhs = isb[:, 4:5] if c < NW else isb[:, 5:6]
                v.tensor_scalar(sgb[:, :], wf32[:, c % 2, :], nhs, None,
                                ALU.is_le)
                v.scalar_tensor_tensor(wchk[:, c % 2, :], wf32[:, c % 2, :],
                                       hs, sgb[:, :],
                                       ALU.is_ge, ALU.subtract
                                       ).then_inc(S.VWB, 1)

            def wt_evac(c):
                v.wait_ge(S.TWT, c + 1)
                if c < NW:
                    dst = w1T[:, :, c * 128:(c + 1) * 128]
                else:
                    c2 = c - NW
                    r, q = c2 // 4, c2 % 4
                    dst = w2T[:, q * 8:(q + 1) * 8, r * 128:(r + 1) * 128]
                v.tensor_copy(dst,
                              wtps[:, :].rearrange("p (k c) -> p k c", k=8)
                              ).then_inc(S.VWT, 1)

            for t in range(NT):
                v.wait_ge(S.TQ, t + 1)
                if t >= 2:
                    v.wait_ge(S.TS, t - 1)
                v.tensor_copy(qk_sb[:, t % 2, :], qkps[:, t % 2, :]
                              ).then_inc(S.VQ, 1)
                if t >= 1:
                    v.wait_ge(S.TS, t)
                    if t == 1:
                        v.tensor_copy(sc_acc[:, :], sps[:, 0:256]
                                      ).then_inc(S.VSA, 1)
                    else:
                        v.tensor_add(sc_acc[:, :], sc_acc[:, :], sps[:, 0:256]
                                     ).then_inc(S.VSA, 1)
                for _i in range(NRED_PT):
                    if nred < 2 * NW:
                        c = nred
                        v.wait_ge(S.LWA, 16 * (c // 2 + 1))
                        v.reduce_sum(wpart[:, c:c + 1], wf32[:, c % 2, :],
                                     axis=AX.X, apply_absolute_value=True
                                     ).then_inc(S.VWA, 1)
                        nred += 1
                        if c == NW - 1:
                            v.reduce_sum(wpart[:, 64:65], wpart[:, 0:NW],
                                         axis=AX.X).then_inc(S.VWA, 1)
                        if c == 2 * NW - 1:
                            v.reduce_sum(wpart[:, 65:66], wpart[:, NW:2 * NW],
                                         axis=AX.X).then_inc(S.VWA, 1)
                if t == T_SSC:
                    v.wait_ge(S.TSM, 1)
                    v.tensor_scalar(ssc[0:1, 0:1], sps[0:1, 300:301],
                                    WMEAN, 1e-5, ALU.mult, ALU.max)
                    v.tensor_scalar(ssc[0:1, 1:2], sps[0:1, 301:302],
                                    WMEAN, 1e-5, ALU.mult, ALU.max
                                    ).then_inc(S.VSC, 1)
                if t == T_ISB:
                    v.wait_ge(S.TSM, 2)
                    v.tensor_copy(isb[:, 2:3], sps[:, 302:303])
                    v.tensor_copy(isb[:, 3:4], sps[:, 303:304])
                    v.tensor_scalar(isb[:, 0:1], isb[:, 2:3], 0.5, None,
                                    ALU.mult)
                    v.tensor_scalar(isb[:, 1:2], isb[:, 3:4], 0.5, None,
                                    ALU.mult)
                    v.tensor_scalar(isb[:, 4:5], isb[:, 0:1], -1.0, None,
                                    ALU.mult)
                    v.tensor_scalar(isb[:, 5:6], isb[:, 1:2], -1.0, None,
                                    ALU.mult).then_inc(S.VSC, 1)
                if t >= T_TERN + 1:
                    for _i in range(NTERN_PT + 1):
                        if (nev < 2 * NW and nev < nge
                                and nev < (t - T_TERN) * NTERN_PT + 1):
                            wt_evac(nev)
                            nev += 1
                if t >= T_TERN:
                    for _i in range(NTERN_PT):
                        if nge < 2 * NW and nge < (t - T_TERN + 1) * NTERN_PT:
                            tern_ge(nge)
                            nge += 1
            v.wait_ge(S.TS, 64)
            v.tensor_add(sc_acc[:, :], sc_acc[:, :], sps[:, 0:256]
                         ).then_inc(S.VSA, 1)
            while nge < 2 * NW or nev < 2 * NW:
                if nev < 2 * NW and nev <= nge - 1:
                    wt_evac(nev)
                    nev += 1
                if nge < 2 * NW:
                    tern_ge(nge)
                    nge += 1
            v.reduce_max(smax[:, 0:4].rearrange("p (g o) -> p g o", o=1),
                         sc_acc[:, :].rearrange("p (g e) -> p g e", g=4),
                         axis=AX.X)
            v.tensor_scalar(smax[:, 4:8], smax[:, 0:4], -0.125, None,
                            ALU.mult).then_inc(S.VSM1, 1)
            v.wait_ge(S.SSM, 4)
            v.reduce_sum(smax[:, 8:12].rearrange("p (g o) -> p g o", o=1),
                         sc_acc[:, :].rearrange("p (g e) -> p g e", g=4),
                         axis=AX.X)
            v.reciprocal(smax[:, 12:16], smax[:, 8:12])
            mm = None
            for cs in range(4):
                mm = v.tensor_scalar(attn_sb[:, cs * 64:(cs + 1) * 64],
                                     sc_acc[:, cs * 64:(cs + 1) * 64],
                                     smax[:, 12 + cs:13 + cs], None, ALU.mult)
            mm.then_inc(S.SM, 1)

        @block.scalar
        def _(sc):
            sc = SW.bind(sc)
            for t in range(NT):
                if t < 16 and t % 4 == 3:
                    w = t // 4
                    sc.wait_ge(S.LS, 16)
                    for j in range(4):
                        sc.wait_ge(S.TVJ, 4 * w + j + 1)
                        sc.activation(vt[:, j, w * 512:(w + 1) * 512],
                                      vps[:, j % 2, :], AF.Identity,
                                      bias=bqv[:, j:j + 1]).then_inc(S.SV, 1)
            sc.wait_ge(S.VSM1, 1)
            for cs in range(4):
                sc.activation(sc_acc[:, cs * 64:(cs + 1) * 64],
                              sc_acc[:, cs * 64:(cs + 1) * 64],
                              AF.Exp, bias=smax[:, 4 + cs:5 + cs],
                              scale=0.125).then_inc(S.SSM, 1)

        actx.close()
        pctx.close()

        bctx = ExitStack()
        Eb = bctx.enter_context
        wfc_sb = Eb(nc.sbuf_tensor("wfc_sb", [128, 4, 1024], BF16))
        bfc_bf = Eb(nc.sbuf_tensor("bfc_bf", [1, D], BF16))
        gamma_r = Eb(nc.sbuf_tensor("gamma_r", [1, D], BF16))
        gamma_b = Eb(nc.sbuf_tensor("gamma_b", [128, 1024], BF16))
        beta_b = Eb(nc.sbuf_tensor("beta_b", [128, 1024], BF16))
        xo = Eb(nc.sbuf_tensor("xo", [128, 2, 1024], BF16))
        ao_sb = Eb(nc.sbuf_tensor("ao_sb", [128, 512], BF16))
        t_a = Eb(nc.sbuf_tensor("t_a", [128, 3, 1024], F32))
        t_b = Eb(nc.sbuf_tensor("t_b", [128, 1024], BF16))
        x1_sb2 = Eb(nc.sbuf_tensor("x1_sb2", [128, 2, 1024], BF16))
        xqT_sb = Eb(nc.sbuf_tensor("xqT_sb", [128, 1024], BF16))
        g_sb = Eb(nc.sbuf_tensor("g_sb", [128, 4096], F32))

        pbctx = ExitStack()
        Epb = pbctx.enter_context
        aops = Epb(nc.psum_tensor("aops", [128, 512], F32))
        fcps = Epb(nc.psum_tensor("fcps", [128, 2, 512], F32))
        xtps = Epb(nc.psum_tensor("xtps", [128, 1024], BF16))
        hps = Epb(nc.psum_tensor("hps", [128, 2, 512], F32))
        o2ps = Epb(nc.psum_tensor("o2ps", [128, 2, 512], F32))

        JOIN = (("TS", 64), ("TVJ", 16), ("TWT", 64), ("VWT", 64),
                ("VSA", 64), ("SM", 1), ("VWA", 66),
                ("SV", 16), ("VWB", 64), ("VQ", 64),
                ("TQ", 64), ("LWB", 16 * 32))

        def join(eng):
            for nm, val in JOIN:
                eng.wait_ge(sems[nm], val)

        @block.gpsimd
        def _(g):
            g = GW.bind(g)
            join(g)
            g.wait_ge(S.LM, 32)
            g.dma_start(out=gamma_r[:, :], in_=gamma[:, :]).then_inc(S.LM, 16)
            g.wait_ge(S.LM, 48)
            g.wait_ge(S.TGB, 1)
            g.dma_start(out=gamma_r[:, :], in_=beta[:, :]).then_inc(S.LM, 16)
            g.wait_ge(S.LM, 64)
            g.dma_start(out=bfc_bf[:, :], in_=b_fc[:, :]).then_inc(S.LM, 16)
            g.wait_ge(S.LM, 80)
            g.dma_start(out=wfc_sb[:, :, :],
                        in_=w_fc.rearrange("(j p) c -> p j c", p=128)
                        ).then_inc(S.LM, 16)
            def g_rounds(j):
                taj = t_a[:, j % 3, :]
                g.wait_ge(S.VCC, j + 1)
                for k in range(8):
                    m, h = k // 2, k % 2
                    if h == 0 and j >= 1 and 4 * j + m - 3 >= 1:
                        g.wait_ge(S.TGT, 4 * j + m - 3)
                    if 8 * j + k - 1 >= 1:
                        g.wait_ge(S.VGQ, 8 * j + k - 1)
                    g.tensor_scalar(
                        taj[:, h * 512:(h + 1) * 512],
                        g_sb[:, m * 1024 + h * 512:m * 1024 + (h + 1) * 512],
                        stat[:, 14:15], MAGIC, ALU.mult, ALU.add
                        ).then_inc(S.GRD, 1)

            for i in range(NO):
                if i >= 2:
                    g.wait_ge(S.VS2, i - 1)
                if i >= 1:
                    g.wait_ge(S.LO, 16 * i)
                g.dma_start(out=xo[:, i % 2, :],
                            in_=x_own[i * 128:(i + 1) * 128, :]
                            ).then_inc(S.LO, 16)
                if i >= 1:
                    g_rounds(i - 1)
            g_rounds(NO - 1)

        @block.tensor
        def _(te):
            join(te)
            te.wait_ge(S.LM, 48)
            mm = None
            for half in range(2):
                mm = te.matmul(fcps[:, half, :], ones_bf[0:1, 0:128],
                               gamma_r[0:1, half * 512:(half + 1) * 512],
                               start=True, stop=True)
            mm.then_inc(S.TGB, 1)
            te.wait_ge(S.VGB, 1)
            te.wait_ge(S.LM, 80)
            mm = None
            for half in range(2):
                mm = te.matmul(fcps[:, half, :], ones_bf[0:1, 0:128],
                               gamma_r[0:1, half * 512:(half + 1) * 512],
                               start=True, stop=True)
            mm.then_inc(S.TGB, 1)
            te.wait_ge(S.LM, 96)

            def emit_ao(i):
                te.wait_ge(S.SM, 1)
                if i >= 1:
                    te.wait_ge(S.VAO, i)
                mm = None
                for h in range(8):
                    pb = (h % 2) * 64
                    mm = te.matmul(
                        aops[pb:pb + 64, (h // 2) * 128:(h // 2) * 128 + 128],
                        attn_sb[pb:pb + 64, (h // 2) * 64:(h // 2) * 64 + 64],
                        vt[pb:pb + 64, h // 2, i * 128:(i + 1) * 128],
                        start=True, stop=True, tile_position=(pb, pb))
                mm.then_inc(S.TAO, 1)

            def emit_fc(i):
                te.wait_ge(S.VAO, i + 1)
                if i == 0:
                    te.wait_ge(S.VGB, 2)
                else:
                    te.wait_ge(S.VR, i)
                mm = None
                for half in range(2):
                    for j in range(4):
                        te.matmul(fcps[:, half, :],
                                  ao_sb[:, j * 128:(j + 1) * 128],
                                  wfc_sb[:, j, half * 512:(half + 1) * 512],
                                  start=(j == 0), stop=False)
                    mm = te.matmul(fcps[:, half, :], ones_bf[0:1, 0:128],
                                   bfc_bf[0:1, half * 512:(half + 1) * 512],
                                   start=False, stop=True)
                mm.then_inc(S.TFC, 1)

            emit_ao(0)
            emit_fc(0)
            for i in range(NO):
                if i + 1 < NO:
                    emit_ao(i + 1)
                te.wait_ge(S.VX1, i + 1)
                if i >= 1:
                    te.wait_ge(S.VGT, 4 * i)
                mm = None
                for kb in range(8):
                    mm = te.transpose(
                        xtps[:, kb * 128:(kb + 1) * 128],
                        t_b[:, kb * 128:(kb + 1) * 128],
                        ident[:])
                mm.then_inc(S.TXQ, 1)
                te.wait_ge(S.VXQ, i + 1)
                if i == 0:
                    te.wait_ge(S.VWT, 32)
                if i + 1 < NO:
                    emit_fc(i + 1)
                for ob in range(8):
                    if 8 * i + ob >= 2:
                        te.wait_ge(S.SG, 8 * i + ob - 1)
                    mm = None
                    for k in range(8):
                        mm = te.matmul(hps[:, ob % 2, :],
                                       xqT_sb[:, k * 128:(k + 1) * 128],
                                       w1T[:, k, ob * 512:(ob + 1) * 512],
                                       start=(k == 0), stop=(k == 7))
                    mm.then_inc(S.TH, 1)
                for b2 in range(4):
                    te.wait_ge(S.VGQ, 8 * i + 2 * b2 + 2)
                    te.wait_ge(S.VGT, 4 * i + b2)
                    mm = None
                    for kb in range(8):
                        blk = b2 * 8 + kb
                        m_ch, kb_in = blk // 8, blk % 8
                        mm = te.transpose(
                            xtps[:, kb * 128:(kb + 1) * 128],
                            g_sb[:, m_ch * 1024 + kb_in * 64:
                                 m_ch * 1024 + (kb_in + 1) * 64].bitcast(BF16),
                            ident[:])
                    mm.then_inc(S.TGT, 1)
                if i == 0:
                    te.wait_ge(S.VWT, 64)
                else:
                    te.wait_ge(S.VRC, i)
                mm = None
                for half in range(2):
                    for k in range(32):
                        b2k, kbk = k // 8, k % 8
                        if kbk == 0:
                            te.wait_ge(S.VGT, 4 * i + b2k + 1)
                        gqT_blk = g_sb[:, b2k * 1024 + 512 + kbk * 64:
                                       b2k * 1024 + 512 + (kbk + 1) * 64
                                       ].bitcast(BF16)
                        mm = te.matmul(o2ps[:, half, :],
                                       gqT_blk,
                                       w2T[:, k, half * 512:(half + 1) * 512],
                                       start=(k == 0), stop=(k == 31))
                mm.then_inc(S.TF2, 1)

        @block.vector
        def _(v):
            v = VW.bind(v)
            join(v)
            v.wait_ge(S.TGB, 1)
            v.tensor_copy(gamma_b[:, :], fcps[:, :, :].rearrange('p a b -> p (a b)')).then_inc(S.VGB, 1)
            v.wait_ge(S.TGB, 2)
            v.tensor_copy(beta_b[:, :], fcps[:, :, :].rearrange('p a b -> p (a b)')).then_inc(S.VGB, 1)
            v.wait_ge(S.TAO, 1)
            v.tensor_copy(ao_sb[:, :], aops[:, :]).then_inc(S.VAO, 1)

            def v_r(j):
                ta = t_a[:, j % 3, :]
                v.wait_ge(S.TFC, j + 1)
                v.wait_ge(S.LO, 16 * (j + 1))
                if j >= 3:
                    v.wait_ge(S.OUT, 16 * (j - 2))
                v.scalar_tensor_tensor(
                    ta, fcps[:, :, :].rearrange('p a b -> p (a b)'),
                    1.0, xo[:, j % 2, :], ALU.mult, ALU.add
                    ).then_inc(S.VR, 1)

            def v_ych(j):
                ta = t_a[:, j % 3, :]
                v.wait_ge(S.SQ1, j + 1)
                emit_lntiny(v, stat, 0, 1, 2, 3, 1e-6).then_inc(S.VS1, 1)
                if j >= 1:
                    v.wait_ge(S.TXQ, j)
                v.tensor_scalar(t_b[:, :], ta, stat[:, 2:3], stat[:, 3:4],
                                ALU.mult, ALU.add)
                v.tensor_mul(t_b[:, :], t_b[:, :], gamma_b[:, :])
                v.tensor_add(t_b[:, :], t_b[:, :], beta_b[:, :])
                v.tensor_add(ta, t_b[:, :], xo[:, j % 2, :]
                             ).then_inc(S.VRB, 1)

            def v_quant(j):
                ta = t_a[:, j % 3, :]
                c11 = 11 if j % 2 == 0 else 23
                v.wait_ge(S.SQ2, j + 1)
                emit_lntiny(v, stat, 4, 5, 6, 7, 1e-5).then_inc(S.VS2, 1)
                v.wait_ge(S.SX1, j + 1)
                v.reduce_max(stat[:, 8:9], x1_sb2[:, j % 2, :], axis=AX.X,
                             apply_absolute_value=True)
                v.tensor_scalar(stat[:, 8:9], stat[:, 8:9], 1e-5, None,
                                ALU.max)
                v.reciprocal(stat[:, 9:10], stat[:, 8:9])
                v.tensor_scalar(stat[:, 10:11], stat[:, 9:10], 127.0, None,
                                ALU.mult)
                v.tensor_mul(stat[:, 36:37], stat[:, 6:7], stat[:, 10:11])
                v.tensor_mul(stat[:, 37:38], stat[:, 7:8], stat[:, 10:11])
                v.tensor_scalar(stat[:, 37:38], stat[:, 37:38], MAGIC, None,
                                ALU.add)
                v.tensor_mul(stat[:, c11:c11 + 1], stat[:, 8:9], isb[:, 2:3])
                v.tensor_scalar(stat[:, c11:c11 + 1], stat[:, c11:c11 + 1],
                                1.0 / 127.0, None, ALU.mult)
                v.tensor_scalar(ta, ta, stat[:, 36:37], stat[:, 37:38],
                                ALU.mult, ALU.add)
                v.tensor_scalar(t_b[:, :], ta, MAGIC, None, ALU.subtract
                                ).then_inc(S.VX1, 1)

            def v_xqevac(j):
                v.wait_ge(S.TXQ, j + 1)
                if j >= 1:
                    v.wait_ge(S.TH, 8 * j)
                v.tensor_copy(xqT_sb[:, :], xtps[:, :]).then_inc(S.VXQ, 1)

            def v_aoevac(j):
                v.wait_ge(S.TAO, j + 1)
                v.tensor_copy(ao_sb[:, :], aops[:, :]).then_inc(S.VAO, 1)

            def v_amax(j, ob):
                v.wait_ge(S.SG, 8 * j + ob + 1)
                v.reduce_max(stat[:, 24 + ob:25 + ob],
                             g_sb[:, ob * 512:(ob + 1) * 512], axis=AX.X,
                             apply_absolute_value=True)

            def v_sc2(j):
                v.reduce_max(stat[:, 12:13], stat[:, 24:32], axis=AX.X,
                             apply_absolute_value=True)
                v.tensor_scalar(stat[:, 12:13], stat[:, 12:13], 1e-5, None,
                                ALU.max)
                v.reciprocal(stat[:, 13:14], stat[:, 12:13])
                v.tensor_scalar(stat[:, 14:15], stat[:, 13:14], 127.0, None,
                                ALU.mult)
                v.tensor_mul(stat[:, 15:16], stat[:, 12:13], isb[:, 3:4])
                v.tensor_scalar(stat[:, 15:16], stat[:, 15:16], 1.0 / 127.0,
                                None, ALU.mult).then_inc(S.VCC, 1)

            def v_evac(j, b2):
                v.wait_ge(S.TGT, 4 * j + b2 + 1)
                if j >= 1:
                    v.wait_ge(S.TF2, j)
                v.tensor_copy(
                    g_sb[:, b2 * 1024 + 512:(b2 + 1) * 1024].bitcast(BF16),
                    xtps[:, :]).then_inc(S.VGT, 1)

            def v_r3(j):
                ta = t_a[:, j % 3, :]
                v.wait_ge(S.TF2, j + 1)
                v.scalar_tensor_tensor(
                    ta, o2ps[:, :, :].rearrange('p a b -> p (a b)'),
                    stat[:, 15:16], x1_sb2[:, j % 2, :], ALU.mult, ALU.add
                    ).then_inc(S.VRC, 1)

            def v_lnt3(j):
                v.wait_ge(S.SQ3, j + 1)
                emit_lntiny(v, stat, 16, 17, 18, 19, 1e-5, sum_c=39,
                            sq_c=38).then_inc(S.VS3, 1)

            def v_out(j):
                ta = t_a[:, j % 3, :]
                v.wait_ge(S.VS3, j + 1)
                if j >= 1:
                    v.wait_ge(S.OUT, 16 * j)
                v.tensor_scalar(ta, ta, stat[:, 18:19], stat[:, 19:20],
                                ALU.mult, ALU.add).then_inc(S.SO, 1)

            v_r(0)
            v_ych(0)
            v_quant(0)
            for i in range(NO):
                v_xqevac(i)
                if i + 1 < NO:
                    v_aoevac(i + 1)
                if i >= 1:
                    v_r3(i - 1)
                if i + 1 < NO:
                    v_r(i + 1)
                if i >= 1:
                    v_lnt3(i - 1)
                    v_out(i - 1)
                v_amax(i, 0)
                v_amax(i, 1)
                if i + 1 < NO:
                    v_ych(i + 1)
                v_amax(i, 2)
                v_amax(i, 3)
                for ob in range(4, 8):
                    v_amax(i, ob)
                v_sc2(i)
                for b2 in range(4):
                    v_evac(i, b2)
                if i + 1 < NO:
                    v_quant(i + 1)
            v_r3(NO - 1)
            v_lnt3(NO - 1)
            v_out(NO - 1)

        @block.scalar
        def _(sc):
            sc = SW.bind(sc)
            join(sc)
            def s_gelu(i, ob):
                c11 = 11 if i % 2 == 0 else 23
                sc.wait_ge(S.TH, 8 * i + ob + 1)
                sc.activation(g_sb[:, ob * 512:(ob + 1) * 512],
                              hps[:, ob % 2, :], AF.Gelu,
                              scale=stat[:, c11:c11 + 1]).then_inc(S.SG, 1)

            def s_ln1(i):
                ta = t_a[:, i % 3, :]
                sc.wait_ge(S.VR, i + 1)
                if i >= 1:
                    sc.wait_ge(S.TXQ, i)
                sc.activation(t_b[:, :], ta, AF.Square,
                              accum_out=stat[:, 21:22])
                sc.activation(ta, ta, AF.Identity,
                              accum_out=stat[:, 20:21]).then_inc(S.SQ1, 1)

            def s_ln2(i):
                ta = t_a[:, i % 3, :]
                sc.wait_ge(S.VRB, i + 1)
                sc.activation(t_b[:, :], ta, AF.Square,
                              accum_out=stat[:, 21:22])
                sc.activation(ta, ta, AF.Identity,
                              accum_out=stat[:, 20:21]).then_inc(S.SQ2, 1)

            def s_x1(i):
                ta = t_a[:, i % 3, :]
                sc.wait_ge(S.VS2, i + 1)
                if i >= 2:
                    sc.wait_ge(S.VRC, i - 1)
                sc.activation(x1_sb2[:, i % 2, :], ta, AF.Identity,
                              bias=stat[:, 7:8], scale=stat[:, 6:7]
                              ).then_inc(S.SX1, 1)

            def s_ln3(i):
                ta = t_a[:, i % 3, :]
                sc.wait_ge(S.VRC, i + 1)
                sc.activation(x1_sb2[:, i % 2, :], ta, AF.Square,
                              accum_out=stat[:, 38:39])
                sc.activation(ta, ta, AF.Identity,
                              accum_out=stat[:, 39:40]).then_inc(S.SQ3, 1)

            def s_sub(i, k):
                m, h = k // 2, k % 2
                sc.wait_ge(S.GRD, 8 * i + k + 1)
                sc.activation(
                    g_sb[:, m * 1024 + h * 256:m * 1024 + (h + 1) * 256
                         ].bitcast(BF16),
                    t_a[:, i % 3, h * 512:(h + 1) * 512], AF.Identity,
                    bias=cmag[:, 1:2]).then_inc(S.VGQ, 1)

            s_ln1(0)
            s_ln2(0)
            s_x1(0)
            for i in range(NO):
                if i >= 1:
                    s_ln3(i - 1)
                s_gelu(i, 0)
                s_gelu(i, 1)
                if i + 1 < NO:
                    s_ln1(i + 1)
                for ob in range(2, 8):
                    s_gelu(i, ob)
                for k in range(4):
                    s_sub(i, k)
                if i + 1 < NO:
                    s_ln2(i + 1)
                for k in range(4, 8):
                    s_sub(i, k)
                if i + 1 < NO:
                    s_x1(i + 1)
            s_ln3(NO - 1)

        @block.sync
        def _(s):
            for i in range(NO):
                s.wait_ge(S.SO, i + 1)
                s.dma_start(out=y[i * 128:(i + 1) * 128, :],
                            in_=t_a[:, i % 3, :]).then_inc(S.OUT, 16)
            s.wait_ge(S.OUT, 16 * NO)

        bctx.close()
        pbctx.close()

    return nc


_programs = {}


def kernel(L0_attn_x, L0_attn_w_qkv, L0_attn_b_qkv, L0_attn_w_fc,
           L0_attn_b_fc, L0_attn_gamma, L0_attn_beta, L0_ffn_w1, L0_ffn_w2):
    x = np.ascontiguousarray(np.asarray(L0_attn_x, dtype=np.float32))
    w_qkv = np.ascontiguousarray(np.asarray(L0_attn_w_qkv, dtype=np.float32))
    b_qkv = np.asarray(L0_attn_b_qkv, dtype=np.float32).reshape(1, 1536)
    w_fc = np.ascontiguousarray(np.asarray(L0_attn_w_fc, dtype=np.float32))
    b_fc = np.asarray(L0_attn_b_fc, dtype=np.float32).reshape(1, D)
    gamma = np.asarray(L0_attn_gamma, dtype=np.float32).reshape(1, D)
    beta = np.asarray(L0_attn_beta, dtype=np.float32).reshape(1, D)
    w1 = np.ascontiguousarray(np.asarray(L0_ffn_w1, dtype=np.float32))
    w2 = np.ascontiguousarray(np.asarray(L0_ffn_w2, dtype=np.float32))
    bqv = np.ascontiguousarray(b_qkv[0, 1024:1536].reshape(4, 128).T)

    nc = _programs.get("nc")
    if nc is None:
        nc = build_program()
        _programs["nc"] = nc

    in_maps = []
    for c in range(8):
        b, qt = c // 4, c % 4
        xb = x[b]
        rolled = np.concatenate([xb[qt * 2048:(qt + 1) * 2048],
                                 xb[:qt * 2048],
                                 xb[(qt + 1) * 2048:]], axis=0)
        in_maps.append({
            "xTt": np.ascontiguousarray(
                rolled.reshape(64, 128, 8, 128).transpose(0, 3, 2, 1)
                .reshape(NT * 128, D)),
            "x_own": np.ascontiguousarray(xb[qt * 2048:(qt + 1) * 2048]),
            "w_qkv": w_qkv, "b_qkv": b_qkv, "bqv_in": bqv,
            "w_fc": w_fc, "b_fc": b_fc, "gamma": gamma, "beta": beta,
            "w1": w1, "w2": w2,
        })

    res = run_bass_kernel_spmd(nc, in_maps, core_ids=list(range(8)))
    out = np.empty((2, 8192, D), dtype=np.float32)
    for c in range(8):
        b, qt = c // 4, c % 4
        out[b, qt * 2048:(qt + 1) * 2048] = np.asarray(res.results[c]["y"])
    return out


# version 23
# speedup vs baseline: 1.1200x; 1.1200x over previous
import numpy as np
from contextlib import ExitStack

import concourse.bass as bass
import concourse.mybir as mybir
from concourse.bass_utils import run_bass_kernel_spmd

F32 = mybir.dt.float32
BF16 = mybir.dt.bfloat16
FP16 = mybir.dt.float16
I32 = mybir.dt.int32
AF = mybir.ActivationFunctionType
ALU = mybir.AluOpType
AX = mybir.AxisListType

MAGIC = 12582912.0
D = 1024
NT = 64
NO = 16
HID = 4096
NW = 32
WMEAN = 1.0 / (HID * D)

NRED_PT = 2
T_SUM = 35
T_SSC = 36
T_BC = 37
T_ISB = 38
T_TERN = 39
NTERN_PT = 2



_DATA_OPS = {
    "tensor_scalar", "tensor_tensor", "tensor_copy", "tensor_add",
    "tensor_sub", "tensor_mul", "tensor_max", "tensor_single_scalar",
    "tensor_scalar_add", "tensor_scalar_sub", "tensor_scalar_mul",
    "tensor_scalar_max", "tensor_scalar_min", "scalar_tensor_tensor",
    "reduce_sum", "reduce_max", "tensor_reduce", "reciprocal", "memset",
    "affine_select", "activation", "copy", "add", "mul", "sign", "square",
    "bn_stats", "bn_aggr", "select",
}


class _Pending:
    def __init__(self, ser):
        self.ser = ser

    def then_inc(self, sem, amt=1):
        self.ser._milestone(sem, amt)
        return self


class Ser:

    def __init__(self, selfsem, counts):
        self.e = None
        self.s = selfsem
        self.n = 0
        self.counts = counts
        self.pending = None
        self.last = None

    def bind(self, eng):
        self.e = eng
        return self

    def _flush(self):
        if self.pending is not None:
            self.n += 1
            self.pending.then_inc(self.s, 1)
            self.last = (self.s, self.n)
            self.pending = None

    def _milestone(self, sem, amt):
        assert self.pending is not None
        v = self.counts.get(id(sem), 0) + amt
        self.counts[id(sem)] = v
        self.pending.then_inc(sem, amt)
        self.last = (sem, v)
        self.pending = None

    def wait_ge(self, sem, v):
        self.e.wait_ge(sem, v)

    def dma_start(self, *a, **k):
        self._flush()
        return self.e.dma_start(*a, **k)

    def end(self):
        self._flush()

    def __getattr__(self, name):
        f = getattr(self.e, name)
        if name in _DATA_OPS:
            def g(*a, **k):
                self._flush()
                if self.last is not None:
                    self.e.wait_ge(*self.last)
                self.pending = f(*a, **k)
                return _Pending(self)
            return g
        return f


def emit_rsqrt(v, out, var, eps, scr):
    x = scr[:, 0:1]
    t = scr[:, 1:2]
    v.tensor_scalar(x, var, float(eps), None, ALU.add)
    v.tensor_scalar(out.bitcast(I32), x.bitcast(I32), 1, None,
                    ALU.logical_shift_right)
    v.tensor_scalar(out.bitcast(I32), out.bitcast(I32), 0x5F3759DF, -1,
                    ALU.subtract, ALU.mult)
    for _ in range(2):
        v.tensor_mul(t, out, out)
        v.tensor_mul(t, t, x)
        v.tensor_scalar(t, t, -0.5, 1.5, ALU.mult, ALU.add)
        v.tensor_mul(out, out, t)


def emit_lntiny(v, stat, mu_c, var_c, rstd_c, bias_c, eps, sum_c=20, sq_c=21):
    v.tensor_scalar(stat[:, mu_c:mu_c + 1], stat[:, sum_c:sum_c + 1],
                    1.0 / 1024.0, None, ALU.mult)
    v.tensor_mul(stat[:, 22:23], stat[:, mu_c:mu_c + 1],
                 stat[:, mu_c:mu_c + 1])
    v.scalar_tensor_tensor(stat[:, var_c:var_c + 1], stat[:, sq_c:sq_c + 1],
                           1.0 / 1024.0, stat[:, 22:23], ALU.mult,
                           ALU.subtract)
    emit_rsqrt(v, stat[:, rstd_c:rstd_c + 1], stat[:, var_c:var_c + 1],
               eps, stat[:, 32:34])
    v.tensor_mul(stat[:, bias_c:bias_c + 1], stat[:, mu_c:mu_c + 1],
                 stat[:, rstd_c:rstd_c + 1])
    return v.tensor_scalar(stat[:, bias_c:bias_c + 1],
                           stat[:, bias_c:bias_c + 1], -1.0, None, ALU.mult)


def build_program():
    nc = bass.Bass("TRN2", target_bir_lowering=False)

    xTt = nc.dram_tensor("xTt", [NT * 128, D], F32, kind="ExternalInput").ap()
    x_own = nc.dram_tensor("x_own", [2048, D], F32, kind="ExternalInput").ap()
    w_qkv = nc.dram_tensor("w_qkv", [D, 1536], F32, kind="ExternalInput").ap()
    b_qkv = nc.dram_tensor("b_qkv", [1, 1536], F32, kind="ExternalInput").ap()
    bqv_in = nc.dram_tensor("bqv_in", [128, 4], F32, kind="ExternalInput").ap()
    w_fc = nc.dram_tensor("w_fc", [512, D], F32, kind="ExternalInput").ap()
    b_fc = nc.dram_tensor("b_fc", [1, D], F32, kind="ExternalInput").ap()
    gamma = nc.dram_tensor("gamma", [1, D], F32, kind="ExternalInput").ap()
    beta = nc.dram_tensor("beta", [1, D], F32, kind="ExternalInput").ap()
    w1 = nc.dram_tensor("w1", [HID, D], F32, kind="ExternalInput").ap()
    w2 = nc.dram_tensor("w2", [D, HID], F32, kind="ExternalInput").ap()
    y = nc.dram_tensor("y", [2048, D], F32, kind="ExternalOutput").ap()

    ctx = ExitStack()
    with ctx:
        E = ctx.enter_context
        w1T = E(nc.sbuf_tensor("w1T", [128, 8, HID], BF16))
        w2T = E(nc.sbuf_tensor("w2T", [128, 32, D], BF16))
        vt = E(nc.sbuf_tensor("vt", [128, 4, 2048], BF16))
        ident = E(nc.sbuf_tensor("ident", [128, 128], BF16))
        ones_bf = E(nc.sbuf_tensor("ones_bf", [1, 512], BF16))
        ones_col = E(nc.sbuf_tensor("ones_col", [128, 1], F32))
        ones_rowf = E(nc.sbuf_tensor("ones_rowf", [1, 128], F32))
        cmag = E(nc.sbuf_tensor("cmag", [128, 2], F32))
        bqkv_bf = E(nc.sbuf_tensor("bqkv_bf", [1, 1536], BF16))
        bqv = E(nc.sbuf_tensor("bqv", [128, 4], F32))
        attn_sb = E(nc.sbuf_tensor("attn_sb", [128, 256], BF16))
        smax = E(nc.sbuf_tensor("smax", [128, 16], F32))
        ssc = E(nc.sbuf_tensor("ssc", [1, 8], F32))
        isb = E(nc.sbuf_tensor("isb", [128, 6], F32))
        stat = E(nc.sbuf_tensor("stat", [128, 40], F32))

        sems = {}
        for name in ("GP", "VCON", "VSELF", "SSELF", "GSELF",
                     "LM", "LQ", "LO", "LWA", "LWB", "LS",
                     "TQ", "TS", "TVJ", "TWT", "TSM", "TGB",
                     "VQ", "VSA", "VWA", "VWB", "VWT", "VSC", "VSM1",
                     "SV", "SSM", "SWR", "SM",
                     "TAO", "TFC", "TXQ", "TH", "TGT", "TF2",
                     "VAO", "VR", "VS1", "VS2", "VQ1", "VX1", "VXQ",
                     "VQ2", "VGQ", "VGT", "VS3", "VGB", "VCC", "GRD",
                     "LQ0", "LQ1", "LQ2", "LQ3", "GAB",
                     "LWA0", "LWA1", "LWB0", "LWB1",
                     "SY", "SX1", "SXQ", "SG", "SR", "SO", "OUT",
                     "VRB", "VRC", "SQ1", "SQ2", "SQ3"):
            sems[name] = E(nc.semaphore(name))
        S = type("S", (), sems)

        actx = ExitStack()
        Ea = actx.enter_context
        wqkv_sb = Ea(nc.sbuf_tensor("wqkv_sb", [128, 8, 1536], BF16))
        sc_acc = Ea(nc.sbuf_tensor("sc_acc", [128, 256], F32))
        ring = Ea(nc.sbuf_tensor("ring", [128, 4, 1024], BF16))
        qk_sb = Ea(nc.sbuf_tensor("qk_sb", [128, 2, 1024], BF16))
        wchk = Ea(nc.sbuf_tensor("wchk", [128, 2, 1024], BF16))
        wf32 = Ea(nc.sbuf_tensor("wf32", [128, 2, 2, 1024], F32))
        wpart = Ea(nc.sbuf_tensor("wpart", [128, 68], F32))

        pctx = ExitStack()
        Ep = pctx.enter_context
        qkps = Ep(nc.psum_tensor("qkps", [128, 2, 1024], F32))
        sps = Ep(nc.psum_tensor("sps", [128, 512], F32))
        vps = Ep(nc.psum_tensor("vps", [128, 2, 512], F32))
        wtps = Ep(nc.psum_tensor("wtps", [128, 1024], BF16))

        block = E(nc.Block())

        _counts = {}
        VW = Ser(sems["VSELF"], _counts)
        SW = Ser(sems["SSELF"], _counts)
        GW = Ser(sems["GSELF"], _counts)

        w1v = w1.rearrange("(r p) i -> p r i", p=128)
        w2v = w2.rearrange("(r p) (q i) -> p r q i", p=128, i=1024)

        def wsrc(c):
            if c < NW:
                return w1v[:, c, :]
            c2 = c - NW
            return w2v[:, c2 // 4, c2 % 4, :]

        def wsrc_pair(p):
            c = 2 * p
            if c < NW:
                return w1v[:, c:c + 2, :]
            c2 = c - NW
            return w2v[:, c2 // 4, (c2 % 4):(c2 % 4) + 2, :]

        @block.gpsimd
        def _(g):
            g = GW.bind(g)
            g.memset(ident[:], 0.0)
            g.affine_select(out=ident[:], in_=ident[:],
                            compare_op=ALU.not_equal, fill=1.0, base=0,
                            pattern=[[-1, 128]], channel_multiplier=1,
                            ).then_inc(S.GP, 1)
            g.dma_start(out=bqkv_bf[:, :], in_=b_qkv[:, :]).then_inc(S.LM, 16)
            g.wait_ge(S.LM, 16)
            g.dma_start(out=wqkv_sb[:, :, :],
                        in_=w_qkv.rearrange("(k p) c -> p k c", p=128)
                        ).then_inc(S.LM, 16)
            lqs = [S.LQ0, S.LQ1, S.LQ2, S.LQ3]
            lwas = [S.LWA0, S.LWA1]
            nwa = 0
            for t in range(NT):
                if t >= 4:
                    tr = t - 4
                    g.wait_ge(S.TQ, tr + 1)
                    if tr < 16:
                        g.wait_ge(S.TVJ, 4 * (tr // 4 + 1))
                g.dma_start(
                    out=ring[:, t % 4, :],
                    in_=xTt[t * 128:(t + 1) * 128, :],
                ).then_inc(lqs[t % 4], 16)
                for _i in range(1):
                    if nwa < NW:
                        p = nwa
                        if p >= 2:
                            g.wait_ge(S.VWA,
                                      2 * p - 2 + (1 if p >= 17 else 0))
                        g.dma_start(out=wf32[:, p % 2, :, :], in_=wsrc_pair(p)
                                    ).then_inc(lwas[p % 2], 16)
                        nwa += 1


        @block.sync
        def _(s):
            s.dma_start(out=bqv[:, :], in_=bqv_in[:, :]).then_inc(S.LS, 16)
            lwbs = [S.LWB0, S.LWB1]
            for p in range(NW):
                if p <= 1:
                    s.wait_ge(S.VWA, 66)
                if p >= 2:
                    s.wait_ge(S.VWB, 2 * p - 2)
                s.dma_start(out=wf32[:, p % 2, :, :], in_=wsrc_pair(p)
                            ).then_inc(lwbs[p % 2], 16)

        @block.tensor
        def _(te):
            te.wait_ge(S.LM, 32)
            te.wait_ge(S.GP, 1)
            te.wait_ge(S.VCON, 1)
            nwt = 0

            def tern_transposes(c):
                te.wait_ge(S.VWB, c + 1)
                if c >= 1:
                    te.wait_ge(S.VWT, c)
                mm = None
                for kb in range(8):
                    mm = te.transpose(wtps[:, kb * 128:(kb + 1) * 128],
                                      wchk[:, c % 2, kb * 128:(kb + 1) * 128],
                                      ident[:])
                mm.then_inc(S.TWT, 1)

            lqs = [S.LQ0, S.LQ1, S.LQ2, S.LQ3]
            for t in range(NT):
                te.wait_ge(lqs[t % 4], 16 * (t // 4 + 1))
                if t >= 2:
                    te.wait_ge(S.VQ, t - 1)
                sl = t % 2
                mm = None
                for half in range(2):
                    for k in range(8):
                        te.matmul(qkps[:, sl, half * 512:(half + 1) * 512],
                                  ring[:, t % 4, k * 128:(k + 1) * 128],
                                  wqkv_sb[:, k, half * 512:(half + 1) * 512],
                                  start=(k == 0), stop=False)
                    mm = te.matmul(qkps[:, sl, half * 512:(half + 1) * 512],
                                   ones_bf[0:1, 0:128],
                                   bqkv_bf[0:1, half * 512:(half + 1) * 512],
                                   start=False, stop=True)
                mm.then_inc(S.TQ, 1)
                if t >= 1:
                    tp = t - 1
                    te.wait_ge(S.VQ, tp + 1)
                    if tp >= 1:
                        te.wait_ge(S.VSA, tp)
                    mm = None
                    for h in range(8):
                        pb = (h % 2) * 64
                        mm = te.matmul(
                            sps[pb:pb + 64, (h // 2) * 64:(h // 2) * 64 + 64],
                            qk_sb[:, tp % 2, h * 64:(h + 1) * 64],
                            qk_sb[:, tp % 2, 512 + h * 64:512 + (h + 1) * 64],
                            start=True, stop=True, tile_position=(0, pb))
                    mm.then_inc(S.TS, 1)
                if t < 16 and t % 4 == 3:
                    w = t // 4
                    for j in range(4):
                        if 4 * w + j >= 2:
                            te.wait_ge(S.SV, 4 * w + j - 1)
                        mm = None
                        for k in range(8):
                            mm = te.matmul(
                                vps[:, j % 2, :],
                                wqkv_sb[:, k, 1024 + j * 128:1024 + (j + 1) * 128],
                                ring[:, 0:4, k * 128:(k + 1) * 128],
                                start=(k == 0), stop=(k == 7))
                        mm.then_inc(S.TVJ, 1)
                if t == T_SUM:
                    te.wait_ge(S.GP, 1)
                    te.wait_ge(S.VWA, 66)
                    te.matmul(sps[0:1, 300:301], ones_col[:, 0:1],
                              wpart[:, 64:65], start=True, stop=True)
                    te.matmul(sps[0:1, 301:302], ones_col[:, 0:1],
                              wpart[:, 65:66], start=True, stop=True
                              ).then_inc(S.TSM, 1)
                if t == T_BC:
                    te.wait_ge(S.VSC, 1)
                    te.matmul(sps[:, 302:303], ones_rowf[0:1, :],
                              ssc[0:1, 0:1], start=True, stop=True)
                    te.matmul(sps[:, 303:304], ones_rowf[0:1, :],
                              ssc[0:1, 1:2], start=True, stop=True
                              ).then_inc(S.TSM, 1)
                if t >= T_TERN + 1:
                    for _i in range(NTERN_PT + 1):
                        if nwt < 2 * NW and nwt < (t - T_TERN) * NTERN_PT:
                            tern_transposes(nwt)
                            nwt += 1
            te.wait_ge(S.VQ, 64)
            te.wait_ge(S.VSA, 63)
            mm = None
            for h in range(8):
                pb = (h % 2) * 64
                mm = te.matmul(
                    sps[pb:pb + 64, (h // 2) * 64:(h // 2) * 64 + 64],
                    qk_sb[:, 1, h * 64:(h + 1) * 64],
                    qk_sb[:, 1, 512 + h * 64:512 + (h + 1) * 64],
                    start=True, stop=True, tile_position=(0, pb))
            mm.then_inc(S.TS, 1)
            while nwt < 2 * NW:
                tern_transposes(nwt)
                nwt += 1

        @block.vector
        def _(v):
            v = VW.bind(v)
            v.memset(ones_bf[:, :], 1.0)
            v.memset(ones_col[:, :], 1.0)
            v.memset(ones_rowf[:, :], 1.0)
            v.memset(cmag[:, 0:1], MAGIC)
            v.memset(cmag[:, 1:2], -MAGIC).then_inc(S.VCON, 1)
            nge = 0
            nev = 0
            nred = 0

            def tern_ge(c):
                sgb = vps[:, :, :].rearrange("p a b -> p (a b)")
                v.wait_ge(S.VSC, 2)
                v.wait_ge([S.LWB0, S.LWB1][(c // 2) % 2],
                          16 * ((c // 2) // 2 + 1))
                if c < 2:
                    v.wait_ge(S.VWA, 66)
                    v.wait_ge(S.SV, 16)
                if c >= 2:
                    v.wait_ge(S.TWT, c - 1)
                hs = isb[:, 0:1] if c < NW else isb[:, 1:2]
                nhs = isb[:, 4:5] if c < NW else isb[:, 5:6]
                v.tensor_scalar(sgb[:, :], wf32[:, (c // 2) % 2, c % 2, :],
                                nhs, None, ALU.is_le)
                v.scalar_tensor_tensor(wchk[:, c % 2, :],
                                       wf32[:, (c // 2) % 2, c % 2, :],
                                       hs, sgb[:, :],
                                       ALU.is_ge, ALU.subtract
                                       ).then_inc(S.VWB, 1)

            def wt_evac(c):
                v.wait_ge(S.TWT, c + 1)
                if c < NW:
                    dst = w1T[:, :, c * 128:(c + 1) * 128]
                else:
                    c2 = c - NW
                    r, q = c2 // 4, c2 % 4
                    dst = w2T[:, q * 8:(q + 1) * 8, r * 128:(r + 1) * 128]
                v.tensor_copy(dst,
                              wtps[:, :].rearrange("p (k c) -> p k c", k=8)
                              ).then_inc(S.VWT, 1)

            for t in range(NT):
                v.wait_ge(S.TQ, t + 1)
                if t >= 2:
                    v.wait_ge(S.TS, t - 1)
                v.tensor_copy(qk_sb[:, t % 2, :], qkps[:, t % 2, :]
                              ).then_inc(S.VQ, 1)
                if t >= 1:
                    v.wait_ge(S.TS, t)
                    if t == 1:
                        v.tensor_copy(sc_acc[:, :], sps[:, 0:256]
                                      ).then_inc(S.VSA, 1)
                    else:
                        v.tensor_add(sc_acc[:, :], sc_acc[:, :], sps[:, 0:256]
                                     ).then_inc(S.VSA, 1)
                for _i in range(NRED_PT):
                    if nred < 2 * NW:
                        c = nred
                        v.wait_ge([S.LWA0, S.LWA1][(c // 2) % 2],
                                  16 * ((c // 2) // 2 + 1))
                        v.reduce_sum(wpart[:, c:c + 1],
                                     wf32[:, (c // 2) % 2, c % 2, :],
                                     axis=AX.X, apply_absolute_value=True
                                     ).then_inc(S.VWA, 1)
                        nred += 1
                        if c == NW - 1:
                            v.reduce_sum(wpart[:, 64:65], wpart[:, 0:NW],
                                         axis=AX.X).then_inc(S.VWA, 1)
                        if c == 2 * NW - 1:
                            v.reduce_sum(wpart[:, 65:66], wpart[:, NW:2 * NW],
                                         axis=AX.X).then_inc(S.VWA, 1)
                if t == T_SSC:
                    v.wait_ge(S.TSM, 1)
                    v.tensor_scalar(ssc[0:1, 0:1], sps[0:1, 300:301],
                                    WMEAN, 1e-5, ALU.mult, ALU.max)
                    v.tensor_scalar(ssc[0:1, 1:2], sps[0:1, 301:302],
                                    WMEAN, 1e-5, ALU.mult, ALU.max
                                    ).then_inc(S.VSC, 1)
                if t == T_ISB:
                    v.wait_ge(S.TSM, 2)
                    v.tensor_copy(isb[:, 2:3], sps[:, 302:303])
                    v.tensor_copy(isb[:, 3:4], sps[:, 303:304])
                    v.tensor_scalar(isb[:, 0:1], isb[:, 2:3], 0.5, None,
                                    ALU.mult)
                    v.tensor_scalar(isb[:, 1:2], isb[:, 3:4], 0.5, None,
                                    ALU.mult)
                    v.tensor_scalar(isb[:, 4:5], isb[:, 0:1], -1.0, None,
                                    ALU.mult)
                    v.tensor_scalar(isb[:, 5:6], isb[:, 1:2], -1.0, None,
                                    ALU.mult).then_inc(S.VSC, 1)
                if t >= T_TERN + 1:
                    for _i in range(NTERN_PT + 1):
                        if (nev < 2 * NW and nev < nge
                                and nev < (t - T_TERN) * NTERN_PT + 1):
                            wt_evac(nev)
                            nev += 1
                if t >= T_TERN:
                    for _i in range(NTERN_PT):
                        if nge < 2 * NW and nge < (t - T_TERN + 1) * NTERN_PT:
                            tern_ge(nge)
                            nge += 1
            v.wait_ge(S.TS, 64)
            v.tensor_add(sc_acc[:, :], sc_acc[:, :], sps[:, 0:256]
                         ).then_inc(S.VSA, 1)
            while nge < 2 * NW or nev < 2 * NW:
                if nev < 2 * NW and nev <= nge - 1:
                    wt_evac(nev)
                    nev += 1
                if nge < 2 * NW:
                    tern_ge(nge)
                    nge += 1
            v.reduce_max(smax[:, 0:4].rearrange("p (g o) -> p g o", o=1),
                         sc_acc[:, :].rearrange("p (g e) -> p g e", g=4),
                         axis=AX.X)
            v.tensor_scalar(smax[:, 4:8], smax[:, 0:4], -0.125, None,
                            ALU.mult).then_inc(S.VSM1, 1)
            v.wait_ge(S.SSM, 4)
            v.reduce_sum(smax[:, 8:12].rearrange("p (g o) -> p g o", o=1),
                         sc_acc[:, :].rearrange("p (g e) -> p g e", g=4),
                         axis=AX.X)
            v.reciprocal(smax[:, 12:16], smax[:, 8:12])
            mm = None
            for cs in range(4):
                mm = v.tensor_scalar(attn_sb[:, cs * 64:(cs + 1) * 64],
                                     sc_acc[:, cs * 64:(cs + 1) * 64],
                                     smax[:, 12 + cs:13 + cs], None, ALU.mult)
            mm.then_inc(S.SM, 1)

        @block.scalar
        def _(sc):
            sc = SW.bind(sc)
            for t in range(NT):
                if t < 16 and t % 4 == 3:
                    w = t // 4
                    sc.wait_ge(S.LS, 16)
                    for j in range(4):
                        sc.wait_ge(S.TVJ, 4 * w + j + 1)
                        sc.activation(vt[:, j, w * 512:(w + 1) * 512],
                                      vps[:, j % 2, :], AF.Identity,
                                      bias=bqv[:, j:j + 1]).then_inc(S.SV, 1)
            sc.wait_ge(S.VSM1, 1)
            for cs in range(4):
                sc.activation(sc_acc[:, cs * 64:(cs + 1) * 64],
                              sc_acc[:, cs * 64:(cs + 1) * 64],
                              AF.Exp, bias=smax[:, 4 + cs:5 + cs],
                              scale=0.125).then_inc(S.SSM, 1)

        actx.close()
        pctx.close()

        bctx = ExitStack()
        Eb = bctx.enter_context
        wfc_sb = Eb(nc.sbuf_tensor("wfc_sb", [128, 4, 1024], BF16))
        bfc_bf = Eb(nc.sbuf_tensor("bfc_bf", [1, D], BF16))
        gamma_r = Eb(nc.sbuf_tensor("gamma_r", [1, D], BF16))
        gamma_b = Eb(nc.sbuf_tensor("gamma_b", [128, 1024], BF16))
        beta_b = Eb(nc.sbuf_tensor("beta_b", [128, 1024], BF16))
        xo = Eb(nc.sbuf_tensor("xo", [128, 2, 1024], BF16))
        ao_sb = Eb(nc.sbuf_tensor("ao_sb", [128, 512], BF16))
        t_a = Eb(nc.sbuf_tensor("t_a", [128, 3, 1024], F32))
        t_b = Eb(nc.sbuf_tensor("t_b", [128, 1024], BF16))
        x1_sb2 = Eb(nc.sbuf_tensor("x1_sb2", [128, 2, 1024], BF16))
        xqT_sb = Eb(nc.sbuf_tensor("xqT_sb", [128, 1024], BF16))
        g_sb = Eb(nc.sbuf_tensor("g_sb", [128, 4096], F32))

        pbctx = ExitStack()
        Epb = pbctx.enter_context
        aops = Epb(nc.psum_tensor("aops", [128, 512], F32))
        fcps = Epb(nc.psum_tensor("fcps", [128, 2, 512], F32))
        xtps = Epb(nc.psum_tensor("xtps", [128, 1024], BF16))
        hps = Epb(nc.psum_tensor("hps", [128, 2, 512], F32))
        o2ps = Epb(nc.psum_tensor("o2ps", [128, 2, 512], F32))

        JOIN = (("TS", 64), ("TVJ", 16), ("TWT", 64), ("VWT", 64),
                ("VSA", 64), ("SM", 1), ("VWA", 66),
                ("SV", 16), ("VWB", 64), ("VQ", 64),
                ("TQ", 64), ("LWB0", 16 * 16), ("LWB1", 16 * 16))

        def join(eng):
            for nm, val in JOIN:
                eng.wait_ge(sems[nm], val)

        @block.gpsimd
        def _(g):
            g = GW.bind(g)
            join(g)
            g.wait_ge(S.LM, 32)
            g.dma_start(out=gamma_r[:, :], in_=gamma[:, :]).then_inc(S.LM, 16)
            g.wait_ge(S.LM, 48)
            g.wait_ge(S.TGB, 1)
            g.dma_start(out=gamma_r[:, :], in_=beta[:, :]).then_inc(S.LM, 16)
            g.wait_ge(S.LM, 64)
            g.dma_start(out=bfc_bf[:, :], in_=b_fc[:, :]).then_inc(S.LM, 16)
            g.wait_ge(S.LM, 80)
            g.dma_start(out=wfc_sb[:, :, :],
                        in_=w_fc.rearrange("(j p) c -> p j c", p=128)
                        ).then_inc(S.LM, 16)
            def g_rounds(j):
                taj = t_a[:, j % 3, :]
                g.wait_ge(S.VCC, j + 1)
                for k in range(8):
                    m, h = k // 2, k % 2
                    if h == 0 and j >= 1 and 4 * j + m - 3 >= 1:
                        g.wait_ge(S.TGT, 4 * j + m - 3)
                    if 8 * j + k - 1 >= 1:
                        g.wait_ge(S.VGQ, 8 * j + k - 1)
                    g.tensor_scalar(
                        taj[:, h * 512:(h + 1) * 512],
                        g_sb[:, m * 1024 + h * 512:m * 1024 + (h + 1) * 512],
                        stat[:, 14:15], MAGIC, ALU.mult, ALU.add
                        ).then_inc(S.GRD, 1)

            for i in range(NO):
                if i >= 2:
                    g.wait_ge(S.VS2, i - 1)
                if i >= 1:
                    g.wait_ge(S.LO, 16 * i)
                g.dma_start(out=xo[:, i % 2, :],
                            in_=x_own[i * 128:(i + 1) * 128, :]
                            ).then_inc(S.LO, 16)
                if i >= 1:
                    g_rounds(i - 1)
            g_rounds(NO - 1)

        @block.tensor
        def _(te):
            join(te)
            te.wait_ge(S.LM, 48)
            mm = None
            for half in range(2):
                mm = te.matmul(fcps[:, half, :], ones_bf[0:1, 0:128],
                               gamma_r[0:1, half * 512:(half + 1) * 512],
                               start=True, stop=True)
            mm.then_inc(S.TGB, 1)
            te.wait_ge(S.VGB, 1)
            te.wait_ge(S.LM, 80)
            mm = None
            for half in range(2):
                mm = te.matmul(fcps[:, half, :], ones_bf[0:1, 0:128],
                               gamma_r[0:1, half * 512:(half + 1) * 512],
                               start=True, stop=True)
            mm.then_inc(S.TGB, 1)
            te.wait_ge(S.LM, 96)

            def emit_ao(i):
                te.wait_ge(S.SM, 1)
                if i >= 1:
                    te.wait_ge(S.VAO, i)
                mm = None
                for h in range(8):
                    pb = (h % 2) * 64
                    mm = te.matmul(
                        aops[pb:pb + 64, (h // 2) * 128:(h // 2) * 128 + 128],
                        attn_sb[pb:pb + 64, (h // 2) * 64:(h // 2) * 64 + 64],
                        vt[pb:pb + 64, h // 2, i * 128:(i + 1) * 128],
                        start=True, stop=True, tile_position=(pb, pb))
                mm.then_inc(S.TAO, 1)

            def emit_fc(i):
                te.wait_ge(S.VAO, i + 1)
                if i == 0:
                    te.wait_ge(S.VGB, 2)
                else:
                    te.wait_ge(S.VR, i)
                mm = None
                for half in range(2):
                    for j in range(4):
                        te.matmul(fcps[:, half, :],
                                  ao_sb[:, j * 128:(j + 1) * 128],
                                  wfc_sb[:, j, half * 512:(half + 1) * 512],
                                  start=(j == 0), stop=False)
                    mm = te.matmul(fcps[:, half, :], ones_bf[0:1, 0:128],
                                   bfc_bf[0:1, half * 512:(half + 1) * 512],
                                   start=False, stop=True)
                mm.then_inc(S.TFC, 1)

            emit_ao(0)
            emit_fc(0)
            for i in range(NO):
                if i + 1 < NO:
                    emit_ao(i + 1)
                te.wait_ge(S.VX1, i + 1)
                if i >= 1:
                    te.wait_ge(S.VGT, 4 * i)
                mm = None
                for kb in range(8):
                    mm = te.transpose(
                        xtps[:, kb * 128:(kb + 1) * 128],
                        t_b[:, kb * 128:(kb + 1) * 128],
                        ident[:])
                mm.then_inc(S.TXQ, 1)
                te.wait_ge(S.VXQ, i + 1)
                if i == 0:
                    te.wait_ge(S.VWT, 32)
                if i + 1 < NO:
                    emit_fc(i + 1)
                for ob in range(8):
                    if 8 * i + ob >= 2:
                        te.wait_ge(S.SG, 8 * i + ob - 1)
                    mm = None
                    for k in range(8):
                        mm = te.matmul(hps[:, ob % 2, :],
                                       xqT_sb[:, k * 128:(k + 1) * 128],
                                       w1T[:, k, ob * 512:(ob + 1) * 512],
                                       start=(k == 0), stop=(k == 7))
                    mm.then_inc(S.TH, 1)
                for b2 in range(4):
                    te.wait_ge(S.VGQ, 8 * i + 2 * b2 + 2)
                    te.wait_ge(S.VGT, 4 * i + b2)
                    mm = None
                    for kb in range(8):
                        blk = b2 * 8 + kb
                        m_ch, kb_in = blk // 8, blk % 8
                        mm = te.transpose(
                            xtps[:, kb * 128:(kb + 1) * 128],
                            g_sb[:, m_ch * 1024 + kb_in * 64:
                                 m_ch * 1024 + (kb_in + 1) * 64].bitcast(BF16),
                            ident[:])
                    mm.then_inc(S.TGT, 1)
                if i == 0:
                    te.wait_ge(S.VWT, 64)
                else:
                    te.wait_ge(S.VRC, i)
                mm = None
                for half in range(2):
                    for k in range(32):
                        b2k, kbk = k // 8, k % 8
                        if kbk == 0:
                            te.wait_ge(S.VGT, 4 * i + b2k + 1)
                        gqT_blk = g_sb[:, b2k * 1024 + 512 + kbk * 64:
                                       b2k * 1024 + 512 + (kbk + 1) * 64
                                       ].bitcast(BF16)
                        mm = te.matmul(o2ps[:, half, :],
                                       gqT_blk,
                                       w2T[:, k, half * 512:(half + 1) * 512],
                                       start=(k == 0), stop=(k == 31))
                mm.then_inc(S.TF2, 1)

        @block.vector
        def _(v):
            v = VW.bind(v)
            join(v)
            v.wait_ge(S.TGB, 1)
            v.tensor_copy(gamma_b[:, :], fcps[:, :, :].rearrange('p a b -> p (a b)')).then_inc(S.VGB, 1)
            v.wait_ge(S.TGB, 2)
            v.tensor_copy(beta_b[:, :], fcps[:, :, :].rearrange('p a b -> p (a b)')).then_inc(S.VGB, 1)
            v.wait_ge(S.TAO, 1)
            v.tensor_copy(ao_sb[:, :], aops[:, :]).then_inc(S.VAO, 1)

            def v_r(j):
                ta = t_a[:, j % 3, :]
                v.wait_ge(S.TFC, j + 1)
                v.wait_ge(S.LO, 16 * (j + 1))
                if j >= 3:
                    v.wait_ge(S.OUT, 16 * (j - 2))
                v.scalar_tensor_tensor(
                    ta, fcps[:, :, :].rearrange('p a b -> p (a b)'),
                    1.0, xo[:, j % 2, :], ALU.mult, ALU.add
                    ).then_inc(S.VR, 1)

            def v_ych(j):
                ta = t_a[:, j % 3, :]
                v.wait_ge(S.SQ1, j + 1)
                emit_lntiny(v, stat, 0, 1, 2, 3, 1e-6).then_inc(S.VS1, 1)
                if j >= 1:
                    v.wait_ge(S.TXQ, j)
                v.tensor_scalar(t_b[:, :], ta, stat[:, 2:3], stat[:, 3:4],
                                ALU.mult, ALU.add)
                v.tensor_mul(t_b[:, :], t_b[:, :], gamma_b[:, :])
                v.tensor_add(t_b[:, :], t_b[:, :], beta_b[:, :])
                v.tensor_add(ta, t_b[:, :], xo[:, j % 2, :]
                             ).then_inc(S.VRB, 1)

            def v_quant(j):
                ta = t_a[:, j % 3, :]
                c11 = 11 if j % 2 == 0 else 23
                v.wait_ge(S.SQ2, j + 1)
                emit_lntiny(v, stat, 4, 5, 6, 7, 1e-5).then_inc(S.VS2, 1)
                v.wait_ge(S.SX1, j + 1)
                v.reduce_max(stat[:, 8:9], x1_sb2[:, j % 2, :], axis=AX.X,
                             apply_absolute_value=True)
                v.tensor_scalar(stat[:, 8:9], stat[:, 8:9], 1e-5, None,
                                ALU.max)
                v.reciprocal(stat[:, 9:10], stat[:, 8:9])
                v.tensor_scalar(stat[:, 10:11], stat[:, 9:10], 127.0, None,
                                ALU.mult)
                v.tensor_mul(stat[:, 36:37], stat[:, 6:7], stat[:, 10:11])
                v.tensor_mul(stat[:, 37:38], stat[:, 7:8], stat[:, 10:11])
                v.tensor_scalar(stat[:, 37:38], stat[:, 37:38], MAGIC, None,
                                ALU.add)
                v.tensor_mul(stat[:, c11:c11 + 1], stat[:, 8:9], isb[:, 2:3])
                v.tensor_scalar(stat[:, c11:c11 + 1], stat[:, c11:c11 + 1],
                                1.0 / 127.0, None, ALU.mult)
                v.tensor_scalar(ta, ta, stat[:, 36:37], stat[:, 37:38],
                                ALU.mult, ALU.add)
                v.tensor_scalar(t_b[:, :], ta, MAGIC, None, ALU.subtract
                                ).then_inc(S.VX1, 1)

            def v_xqevac(j):
                v.wait_ge(S.TXQ, j + 1)
                if j >= 1:
                    v.wait_ge(S.TH, 8 * j)
                v.tensor_copy(xqT_sb[:, :], xtps[:, :]).then_inc(S.VXQ, 1)

            def v_aoevac(j):
                v.wait_ge(S.TAO, j + 1)
                v.tensor_copy(ao_sb[:, :], aops[:, :]).then_inc(S.VAO, 1)

            def v_amax(j, ob):
                v.wait_ge(S.SG, 8 * j + ob + 1)
                v.reduce_max(stat[:, 24 + ob:25 + ob],
                             g_sb[:, ob * 512:(ob + 1) * 512], axis=AX.X,
                             apply_absolute_value=True)

            def v_sc2(j):
                v.reduce_max(stat[:, 12:13], stat[:, 24:32], axis=AX.X,
                             apply_absolute_value=True)
                v.tensor_scalar(stat[:, 12:13], stat[:, 12:13], 1e-5, None,
                                ALU.max)
                v.reciprocal(stat[:, 13:14], stat[:, 12:13])
                v.tensor_scalar(stat[:, 14:15], stat[:, 13:14], 127.0, None,
                                ALU.mult)
                v.tensor_mul(stat[:, 15:16], stat[:, 12:13], isb[:, 3:4])
                v.tensor_scalar(stat[:, 15:16], stat[:, 15:16], 1.0 / 127.0,
                                None, ALU.mult).then_inc(S.VCC, 1)

            def v_evac(j, b2):
                v.wait_ge(S.TGT, 4 * j + b2 + 1)
                if j >= 1:
                    v.wait_ge(S.TF2, j)
                v.tensor_copy(
                    g_sb[:, b2 * 1024 + 512:(b2 + 1) * 1024].bitcast(BF16),
                    xtps[:, :]).then_inc(S.VGT, 1)

            def v_r3(j):
                ta = t_a[:, j % 3, :]
                v.wait_ge(S.TF2, j + 1)
                v.scalar_tensor_tensor(
                    ta, o2ps[:, :, :].rearrange('p a b -> p (a b)'),
                    stat[:, 15:16], x1_sb2[:, j % 2, :], ALU.mult, ALU.add
                    ).then_inc(S.VRC, 1)

            def v_lnt3(j):
                v.wait_ge(S.SQ3, j + 1)
                emit_lntiny(v, stat, 16, 17, 18, 19, 1e-5, sum_c=39,
                            sq_c=38).then_inc(S.VS3, 1)

            def v_out(j):
                ta = t_a[:, j % 3, :]
                v.wait_ge(S.VS3, j + 1)
                if j >= 1:
                    v.wait_ge(S.OUT, 16 * j)
                v.tensor_scalar(ta, ta, stat[:, 18:19], stat[:, 19:20],
                                ALU.mult, ALU.add).then_inc(S.SO, 1)

            v_r(0)
            v_ych(0)
            v_quant(0)
            for i in range(NO):
                v_xqevac(i)
                if i + 1 < NO:
                    v_aoevac(i + 1)
                if i >= 1:
                    v_r3(i - 1)
                if i + 1 < NO:
                    v_r(i + 1)
                if i >= 1:
                    v_lnt3(i - 1)
                    v_out(i - 1)
                v_amax(i, 0)
                v_amax(i, 1)
                if i + 1 < NO:
                    v_ych(i + 1)
                v_amax(i, 2)
                v_amax(i, 3)
                for ob in range(4, 8):
                    v_amax(i, ob)
                v_sc2(i)
                for b2 in range(4):
                    v_evac(i, b2)
                if i + 1 < NO:
                    v_quant(i + 1)
            v_r3(NO - 1)
            v_lnt3(NO - 1)
            v_out(NO - 1)

        @block.scalar
        def _(sc):
            sc = SW.bind(sc)
            join(sc)
            def s_gelu(i, ob):
                c11 = 11 if i % 2 == 0 else 23
                sc.wait_ge(S.TH, 8 * i + ob + 1)
                sc.activation(g_sb[:, ob * 512:(ob + 1) * 512],
                              hps[:, ob % 2, :], AF.Gelu,
                              scale=stat[:, c11:c11 + 1]).then_inc(S.SG, 1)

            def s_ln1(i):
                ta = t_a[:, i % 3, :]
                sc.wait_ge(S.VR, i + 1)
                if i >= 1:
                    sc.wait_ge(S.TXQ, i)
                sc.activation(t_b[:, :], ta, AF.Square,
                              accum_out=stat[:, 21:22])
                sc.activation(ta, ta, AF.Identity,
                              accum_out=stat[:, 20:21]).then_inc(S.SQ1, 1)

            def s_ln2(i):
                ta = t_a[:, i % 3, :]
                sc.wait_ge(S.VRB, i + 1)
                sc.activation(t_b[:, :], ta, AF.Square,
                              accum_out=stat[:, 21:22])
                sc.activation(ta, ta, AF.Identity,
                              accum_out=stat[:, 20:21]).then_inc(S.SQ2, 1)

            def s_x1(i):
                ta = t_a[:, i % 3, :]
                sc.wait_ge(S.VS2, i + 1)
                if i >= 2:
                    sc.wait_ge(S.VRC, i - 1)
                sc.activation(x1_sb2[:, i % 2, :], ta, AF.Identity,
                              bias=stat[:, 7:8], scale=stat[:, 6:7]
                              ).then_inc(S.SX1, 1)

            def s_ln3(i):
                ta = t_a[:, i % 3, :]
                sc.wait_ge(S.VRC, i + 1)
                sc.activation(x1_sb2[:, i % 2, :], ta, AF.Square,
                              accum_out=stat[:, 38:39])
                sc.activation(ta, ta, AF.Identity,
                              accum_out=stat[:, 39:40]).then_inc(S.SQ3, 1)

            def s_sub(i, k):
                m, h = k // 2, k % 2
                sc.wait_ge(S.GRD, 8 * i + k + 1)
                sc.activation(
                    g_sb[:, m * 1024 + h * 256:m * 1024 + (h + 1) * 256
                         ].bitcast(BF16),
                    t_a[:, i % 3, h * 512:(h + 1) * 512], AF.Identity,
                    bias=cmag[:, 1:2]).then_inc(S.VGQ, 1)

            s_ln1(0)
            s_ln2(0)
            s_x1(0)
            for i in range(NO):
                if i >= 1:
                    s_ln3(i - 1)
                s_gelu(i, 0)
                s_gelu(i, 1)
                if i + 1 < NO:
                    s_ln1(i + 1)
                for ob in range(2, 8):
                    s_gelu(i, ob)
                if i + 1 < NO:
                    s_ln2(i + 1)
                for k in range(8):
                    s_sub(i, k)
                if i + 1 < NO:
                    s_x1(i + 1)
            s_ln3(NO - 1)

        @block.sync
        def _(s):
            for i in range(NO):
                s.wait_ge(S.SO, i + 1)
                s.dma_start(out=y[i * 128:(i + 1) * 128, :],
                            in_=t_a[:, i % 3, :]).then_inc(S.OUT, 16)
            s.wait_ge(S.OUT, 16 * NO)

        bctx.close()
        pbctx.close()

    return nc


_programs = {}


def kernel(L0_attn_x, L0_attn_w_qkv, L0_attn_b_qkv, L0_attn_w_fc,
           L0_attn_b_fc, L0_attn_gamma, L0_attn_beta, L0_ffn_w1, L0_ffn_w2):
    x = np.ascontiguousarray(np.asarray(L0_attn_x, dtype=np.float32))
    w_qkv = np.ascontiguousarray(np.asarray(L0_attn_w_qkv, dtype=np.float32))
    b_qkv = np.asarray(L0_attn_b_qkv, dtype=np.float32).reshape(1, 1536)
    w_fc = np.ascontiguousarray(np.asarray(L0_attn_w_fc, dtype=np.float32))
    b_fc = np.asarray(L0_attn_b_fc, dtype=np.float32).reshape(1, D)
    gamma = np.asarray(L0_attn_gamma, dtype=np.float32).reshape(1, D)
    beta = np.asarray(L0_attn_beta, dtype=np.float32).reshape(1, D)
    w1 = np.ascontiguousarray(np.asarray(L0_ffn_w1, dtype=np.float32))
    w2 = np.ascontiguousarray(np.asarray(L0_ffn_w2, dtype=np.float32))
    bqv = np.ascontiguousarray(b_qkv[0, 1024:1536].reshape(4, 128).T)

    nc = _programs.get("nc")
    if nc is None:
        nc = build_program()
        _programs["nc"] = nc

    in_maps = []
    for c in range(8):
        b, qt = c // 4, c % 4
        xb = x[b]
        rolled = np.concatenate([xb[qt * 2048:(qt + 1) * 2048],
                                 xb[:qt * 2048],
                                 xb[(qt + 1) * 2048:]], axis=0)
        in_maps.append({
            "xTt": np.ascontiguousarray(
                rolled.reshape(64, 128, 8, 128).transpose(0, 3, 2, 1)
                .reshape(NT * 128, D)),
            "x_own": np.ascontiguousarray(xb[qt * 2048:(qt + 1) * 2048]),
            "w_qkv": w_qkv, "b_qkv": b_qkv, "bqv_in": bqv,
            "w_fc": w_fc, "b_fc": b_fc, "gamma": gamma, "beta": beta,
            "w1": w1, "w2": w2,
        })

    res = run_bass_kernel_spmd(nc, in_maps, core_ids=list(range(8)))
    out = np.empty((2, 8192, D), dtype=np.float32)
    for c in range(8):
        b, qt = c // 4, c % 4
        out[b, qt * 2048:(qt + 1) * 2048] = np.asarray(res.results[c]["y"])
    return out


# version 24 (final)
# speedup vs baseline: 1.1279x; 1.0070x over previous
import numpy as np
from contextlib import ExitStack

import concourse.bass as bass
import concourse.mybir as mybir
from concourse.bass_utils import run_bass_kernel_spmd

F32 = mybir.dt.float32
BF16 = mybir.dt.bfloat16
FP16 = mybir.dt.float16
I32 = mybir.dt.int32
AF = mybir.ActivationFunctionType
ALU = mybir.AluOpType
AX = mybir.AxisListType

MAGIC = 12582912.0
D = 1024
NT = 64
NO = 16
HID = 4096
NW = 32
WMEAN = 1.0 / (HID * D)

NRED_PT = 2
T_SUM = 35
T_SSC = 36
T_BC = 37
T_ISB = 38
T_TERN = 39
NTERN_PT = 2



_DATA_OPS = {
    "tensor_scalar", "tensor_tensor", "tensor_copy", "tensor_add",
    "tensor_sub", "tensor_mul", "tensor_max", "tensor_single_scalar",
    "tensor_scalar_add", "tensor_scalar_sub", "tensor_scalar_mul",
    "tensor_scalar_max", "tensor_scalar_min", "scalar_tensor_tensor",
    "reduce_sum", "reduce_max", "tensor_reduce", "reciprocal", "memset",
    "affine_select", "activation", "copy", "add", "mul", "sign", "square",
    "bn_stats", "bn_aggr", "select",
}


class _Pending:
    def __init__(self, ser):
        self.ser = ser

    def then_inc(self, sem, amt=1):
        self.ser._milestone(sem, amt)
        return self


class Ser:

    def __init__(self, selfsem, counts):
        self.e = None
        self.s = selfsem
        self.n = 0
        self.counts = counts
        self.pending = None
        self.last = None

    def bind(self, eng):
        self.e = eng
        return self

    def _flush(self):
        if self.pending is not None:
            self.n += 1
            self.pending.then_inc(self.s, 1)
            self.last = (self.s, self.n)
            self.pending = None

    def _milestone(self, sem, amt):
        assert self.pending is not None
        v = self.counts.get(id(sem), 0) + amt
        self.counts[id(sem)] = v
        self.pending.then_inc(sem, amt)
        self.last = (sem, v)
        self.pending = None

    def wait_ge(self, sem, v):
        self.e.wait_ge(sem, v)

    def dma_start(self, *a, **k):
        self._flush()
        return self.e.dma_start(*a, **k)

    def end(self):
        self._flush()

    def __getattr__(self, name):
        f = getattr(self.e, name)
        if name in _DATA_OPS:
            def g(*a, **k):
                self._flush()
                if self.last is not None:
                    self.e.wait_ge(*self.last)
                self.pending = f(*a, **k)
                return _Pending(self)
            return g
        return f


def emit_rsqrt(v, out, var, eps, scr):
    x = scr[:, 0:1]
    t = scr[:, 1:2]
    v.tensor_scalar(x, var, float(eps), None, ALU.add)
    v.tensor_scalar(out.bitcast(I32), x.bitcast(I32), 1, None,
                    ALU.logical_shift_right)
    v.tensor_scalar(out.bitcast(I32), out.bitcast(I32), 0x5F3759DF, -1,
                    ALU.subtract, ALU.mult)
    for _ in range(2):
        v.tensor_mul(t, out, out)
        v.tensor_mul(t, t, x)
        v.tensor_scalar(t, t, -0.5, 1.5, ALU.mult, ALU.add)
        v.tensor_mul(out, out, t)


def emit_lntiny(v, stat, mu_c, var_c, rstd_c, bias_c, eps, sum_c=20, sq_c=21):
    v.tensor_scalar(stat[:, mu_c:mu_c + 1], stat[:, sum_c:sum_c + 1],
                    1.0 / 1024.0, None, ALU.mult)
    v.tensor_mul(stat[:, 22:23], stat[:, mu_c:mu_c + 1],
                 stat[:, mu_c:mu_c + 1])
    v.scalar_tensor_tensor(stat[:, var_c:var_c + 1], stat[:, sq_c:sq_c + 1],
                           1.0 / 1024.0, stat[:, 22:23], ALU.mult,
                           ALU.subtract)
    emit_rsqrt(v, stat[:, rstd_c:rstd_c + 1], stat[:, var_c:var_c + 1],
               eps, stat[:, 32:34])
    v.tensor_mul(stat[:, bias_c:bias_c + 1], stat[:, mu_c:mu_c + 1],
                 stat[:, rstd_c:rstd_c + 1])
    return v.tensor_scalar(stat[:, bias_c:bias_c + 1],
                           stat[:, bias_c:bias_c + 1], -1.0, None, ALU.mult)


def build_program():
    nc = bass.Bass("TRN2", target_bir_lowering=False)

    xTt = nc.dram_tensor("xTt", [NT * 128, D], F32, kind="ExternalInput").ap()
    x_own = nc.dram_tensor("x_own", [2048, D], F32, kind="ExternalInput").ap()
    w_qkv = nc.dram_tensor("w_qkv", [D, 1536], F32, kind="ExternalInput").ap()
    b_qkv = nc.dram_tensor("b_qkv", [1, 1536], F32, kind="ExternalInput").ap()
    bqv_in = nc.dram_tensor("bqv_in", [128, 4], F32, kind="ExternalInput").ap()
    w_fc = nc.dram_tensor("w_fc", [512, D], F32, kind="ExternalInput").ap()
    b_fc = nc.dram_tensor("b_fc", [1, D], F32, kind="ExternalInput").ap()
    gamma = nc.dram_tensor("gamma", [1, D], F32, kind="ExternalInput").ap()
    beta = nc.dram_tensor("beta", [1, D], F32, kind="ExternalInput").ap()
    w1 = nc.dram_tensor("w1", [HID, D], F32, kind="ExternalInput").ap()
    w2 = nc.dram_tensor("w2", [D, HID], F32, kind="ExternalInput").ap()
    y = nc.dram_tensor("y", [2048, D], F32, kind="ExternalOutput").ap()

    ctx = ExitStack()
    with ctx:
        E = ctx.enter_context
        w1T = E(nc.sbuf_tensor("w1T", [128, 8, HID], BF16))
        w2T = E(nc.sbuf_tensor("w2T", [128, 32, D], BF16))
        vt = E(nc.sbuf_tensor("vt", [128, 4, 2048], BF16))
        ident = E(nc.sbuf_tensor("ident", [128, 128], BF16))
        ones_bf = E(nc.sbuf_tensor("ones_bf", [1, 512], BF16))
        ones_col = E(nc.sbuf_tensor("ones_col", [128, 1], F32))
        ones_rowf = E(nc.sbuf_tensor("ones_rowf", [1, 128], F32))
        cmag = E(nc.sbuf_tensor("cmag", [128, 2], F32))
        bqkv_bf = E(nc.sbuf_tensor("bqkv_bf", [1, 1536], BF16))
        bqv = E(nc.sbuf_tensor("bqv", [128, 4], F32))
        attn_sb = E(nc.sbuf_tensor("attn_sb", [128, 256], BF16))
        smax = E(nc.sbuf_tensor("smax", [128, 16], F32))
        ssc = E(nc.sbuf_tensor("ssc", [1, 8], F32))
        isb = E(nc.sbuf_tensor("isb", [128, 8], F32))
        stat = E(nc.sbuf_tensor("stat", [128, 40], F32))

        sems = {}
        for name in ("GP", "VCON", "VSELF", "SSELF", "GSELF",
                     "LM", "LQ", "LO", "LWA", "LWB", "LS",
                     "TQ", "TS", "TVJ", "TWT", "TSM", "TGB",
                     "VQ", "VSA", "VWA", "VWB", "VWT", "VSC", "VSM1",
                     "SV", "SSM", "SWR", "SM",
                     "TAO", "TFC", "TXQ", "TH", "TGT", "TF2",
                     "VAO", "VR", "VS1", "VS2", "VQ1", "VX1", "VXQ",
                     "VQ2", "VGQ", "VGT", "VS3", "VGB", "VCC", "GRD",
                     "LQ0", "LQ1", "LQ2", "LQ3", "GAB",
                     "LWA0", "LWA1", "LWB0", "LWB1",
                     "SY", "SX1", "SXQ", "SG", "SR", "SO", "OUT",
                     "VRB", "VRC", "SQ1", "SQ2", "SQ3"):
            sems[name] = E(nc.semaphore(name))
        S = type("S", (), sems)

        actx = ExitStack()
        Ea = actx.enter_context
        wqkv_sb = Ea(nc.sbuf_tensor("wqkv_sb", [128, 8, 1536], BF16))
        sc_acc = Ea(nc.sbuf_tensor("sc_acc", [128, 256], F32))
        ring = Ea(nc.sbuf_tensor("ring", [128, 4, 1024], BF16))
        qk_sb = Ea(nc.sbuf_tensor("qk_sb", [128, 2, 1024], BF16))
        wchk = Ea(nc.sbuf_tensor("wchk", [128, 2, 1024], BF16))
        wf32 = Ea(nc.sbuf_tensor("wf32", [128, 2, 2, 1024], F32))
        wpart = Ea(nc.sbuf_tensor("wpart", [128, 68], F32))

        pctx = ExitStack()
        Ep = pctx.enter_context
        qkps = Ep(nc.psum_tensor("qkps", [128, 2, 1024], F32))
        sps = Ep(nc.psum_tensor("sps", [128, 512], F32))
        vps = Ep(nc.psum_tensor("vps", [128, 2, 512], F32))
        wtps = Ep(nc.psum_tensor("wtps", [128, 1024], BF16))

        block = E(nc.Block())

        _counts = {}
        VW = Ser(sems["VSELF"], _counts)
        SW = Ser(sems["SSELF"], _counts)
        GW = Ser(sems["GSELF"], _counts)

        w1v = w1.rearrange("(r p) i -> p r i", p=128)
        w2v = w2.rearrange("(r p) (q i) -> p r q i", p=128, i=1024)

        def wsrc(c):
            if c < NW:
                return w1v[:, c, :]
            c2 = c - NW
            return w2v[:, c2 // 4, c2 % 4, :]

        def wsrc_pair(p):
            c = 2 * p
            if c < NW:
                return w1v[:, c:c + 2, :]
            c2 = c - NW
            return w2v[:, c2 // 4, (c2 % 4):(c2 % 4) + 2, :]

        @block.gpsimd
        def _(g):
            g = GW.bind(g)
            g.memset(ident[:], 0.0)
            g.affine_select(out=ident[:], in_=ident[:],
                            compare_op=ALU.not_equal, fill=1.0, base=0,
                            pattern=[[-1, 128]], channel_multiplier=1,
                            ).then_inc(S.GP, 1)
            g.dma_start(out=bqkv_bf[:, :], in_=b_qkv[:, :]).then_inc(S.LM, 16)
            g.wait_ge(S.LM, 16)
            g.dma_start(out=wqkv_sb[:, :, :],
                        in_=w_qkv.rearrange("(k p) c -> p k c", p=128)
                        ).then_inc(S.LM, 16)
            lqs = [S.LQ0, S.LQ1, S.LQ2, S.LQ3]
            lwas = [S.LWA0, S.LWA1]
            nwa = 0
            for t in range(NT):
                if t >= 4:
                    tr = t - 4
                    g.wait_ge(S.TQ, tr + 1)
                    if tr < 16:
                        g.wait_ge(S.TVJ, 4 * (tr // 4 + 1))
                g.dma_start(
                    out=ring[:, t % 4, :],
                    in_=xTt[t * 128:(t + 1) * 128, :],
                ).then_inc(lqs[t % 4], 16)
                for _i in range(1):
                    if nwa < NW:
                        p = nwa
                        if p >= 2:
                            g.wait_ge(S.VWA,
                                      2 * p - 2 + (1 if p >= 17 else 0))
                        g.dma_start(out=wf32[:, p % 2, :, :], in_=wsrc_pair(p)
                                    ).then_inc(lwas[p % 2], 16)
                        nwa += 1


        @block.sync
        def _(s):
            s.dma_start(out=bqv[:, :], in_=bqv_in[:, :]).then_inc(S.LS, 16)
            lwbs = [S.LWB0, S.LWB1]
            for p in range(NW):
                if p <= 1:
                    s.wait_ge(S.VWA, 66)
                if p >= 2:
                    s.wait_ge(S.VWB, 2 * p - 2)
                s.dma_start(out=wf32[:, p % 2, :, :], in_=wsrc_pair(p)
                            ).then_inc(lwbs[p % 2], 16)

        @block.tensor
        def _(te):
            te.wait_ge(S.LM, 32)
            te.wait_ge(S.GP, 1)
            te.wait_ge(S.VCON, 1)
            nwt = 0

            def tern_transposes(c):
                te.wait_ge(S.VWB, c + 1)
                if c >= 1:
                    te.wait_ge(S.VWT, c)
                mm = None
                for kb in range(8):
                    mm = te.transpose(wtps[:, kb * 128:(kb + 1) * 128],
                                      wchk[:, c % 2, kb * 128:(kb + 1) * 128],
                                      ident[:])
                mm.then_inc(S.TWT, 1)

            lqs = [S.LQ0, S.LQ1, S.LQ2, S.LQ3]
            for t in range(NT):
                te.wait_ge(lqs[t % 4], 16 * (t // 4 + 1))
                if t >= 2:
                    te.wait_ge(S.VQ, t - 1)
                sl = t % 2
                mm = None
                for half in range(2):
                    for k in range(8):
                        te.matmul(qkps[:, sl, half * 512:(half + 1) * 512],
                                  ring[:, t % 4, k * 128:(k + 1) * 128],
                                  wqkv_sb[:, k, half * 512:(half + 1) * 512],
                                  start=(k == 0), stop=False)
                    mm = te.matmul(qkps[:, sl, half * 512:(half + 1) * 512],
                                   ones_bf[0:1, 0:128],
                                   bqkv_bf[0:1, half * 512:(half + 1) * 512],
                                   start=False, stop=True)
                mm.then_inc(S.TQ, 1)
                if t >= 1:
                    tp = t - 1
                    te.wait_ge(S.VQ, tp + 1)
                    if tp >= 1:
                        te.wait_ge(S.VSA, tp)
                    mm = None
                    for h in range(8):
                        pb = (h % 2) * 64
                        mm = te.matmul(
                            sps[pb:pb + 64, (h // 2) * 64:(h // 2) * 64 + 64],
                            qk_sb[:, tp % 2, h * 64:(h + 1) * 64],
                            qk_sb[:, tp % 2, 512 + h * 64:512 + (h + 1) * 64],
                            start=True, stop=True, tile_position=(0, pb))
                    mm.then_inc(S.TS, 1)
                if t < 16 and t % 4 == 3:
                    w = t // 4
                    for j in range(4):
                        if 4 * w + j >= 2:
                            te.wait_ge(S.SV, 4 * w + j - 1)
                        mm = None
                        for k in range(8):
                            mm = te.matmul(
                                vps[:, j % 2, :],
                                wqkv_sb[:, k, 1024 + j * 128:1024 + (j + 1) * 128],
                                ring[:, 0:4, k * 128:(k + 1) * 128],
                                start=(k == 0), stop=(k == 7))
                        mm.then_inc(S.TVJ, 1)
                if t == T_SUM:
                    te.wait_ge(S.GP, 1)
                    te.wait_ge(S.VWA, 66)
                    te.matmul(sps[0:1, 300:301], ones_col[:, 0:1],
                              wpart[:, 64:65], start=True, stop=True)
                    te.matmul(sps[0:1, 301:302], ones_col[:, 0:1],
                              wpart[:, 65:66], start=True, stop=True
                              ).then_inc(S.TSM, 1)
                if t == T_BC:
                    te.wait_ge(S.VSC, 1)
                    te.matmul(sps[:, 302:303], ones_rowf[0:1, :],
                              ssc[0:1, 0:1], start=True, stop=True)
                    te.matmul(sps[:, 303:304], ones_rowf[0:1, :],
                              ssc[0:1, 1:2], start=True, stop=True
                              ).then_inc(S.TSM, 1)
                if t >= T_TERN + 1:
                    for _i in range(NTERN_PT + 1):
                        if nwt < 2 * NW and nwt < (t - T_TERN) * NTERN_PT:
                            tern_transposes(nwt)
                            nwt += 1
            te.wait_ge(S.VQ, 64)
            te.wait_ge(S.VSA, 63)
            mm = None
            for h in range(8):
                pb = (h % 2) * 64
                mm = te.matmul(
                    sps[pb:pb + 64, (h // 2) * 64:(h // 2) * 64 + 64],
                    qk_sb[:, 1, h * 64:(h + 1) * 64],
                    qk_sb[:, 1, 512 + h * 64:512 + (h + 1) * 64],
                    start=True, stop=True, tile_position=(0, pb))
            mm.then_inc(S.TS, 1)
            while nwt < 2 * NW:
                tern_transposes(nwt)
                nwt += 1

        @block.vector
        def _(v):
            v = VW.bind(v)
            v.memset(ones_bf[:, :], 1.0)
            v.memset(ones_col[:, :], 1.0)
            v.memset(ones_rowf[:, :], 1.0)
            v.memset(cmag[:, 0:1], MAGIC)
            v.memset(cmag[:, 1:2], -MAGIC).then_inc(S.VCON, 1)
            nge = 0
            nev = 0
            nred = 0

            def tern_ge(c):
                sgb = vps[:, :, :].rearrange("p a b -> p (a b)")
                v.wait_ge(S.VSC, 2)
                v.wait_ge([S.LWB0, S.LWB1][(c // 2) % 2],
                          16 * ((c // 2) // 2 + 1))
                if c < 2:
                    v.wait_ge(S.VWA, 66)
                    v.wait_ge(S.SV, 16)
                if c >= 2:
                    v.wait_ge(S.TWT, c - 1)
                hs = isb[:, 0:1] if c < NW else isb[:, 1:2]
                nhs = isb[:, 4:5] if c < NW else isb[:, 5:6]
                v.tensor_scalar(sgb[:, :], wf32[:, (c // 2) % 2, c % 2, :],
                                nhs, None, ALU.is_le)
                v.scalar_tensor_tensor(wchk[:, c % 2, :],
                                       wf32[:, (c // 2) % 2, c % 2, :],
                                       hs, sgb[:, :],
                                       ALU.is_ge, ALU.subtract
                                       ).then_inc(S.VWB, 1)

            def wt_evac(c):
                v.wait_ge(S.TWT, c + 1)
                if c < NW:
                    dst = w1T[:, :, c * 128:(c + 1) * 128]
                else:
                    c2 = c - NW
                    r, q = c2 // 4, c2 % 4
                    dst = w2T[:, q * 8:(q + 1) * 8, r * 128:(r + 1) * 128]
                v.tensor_copy(dst,
                              wtps[:, :].rearrange("p (k c) -> p k c", k=8)
                              ).then_inc(S.VWT, 1)

            for t in range(NT):
                v.wait_ge(S.TQ, t + 1)
                if t >= 2:
                    v.wait_ge(S.TS, t - 1)
                v.tensor_copy(qk_sb[:, t % 2, :], qkps[:, t % 2, :]
                              ).then_inc(S.VQ, 1)
                if t >= 1:
                    v.wait_ge(S.TS, t)
                    if t == 1:
                        v.tensor_copy(sc_acc[:, :], sps[:, 0:256]
                                      ).then_inc(S.VSA, 1)
                    else:
                        v.tensor_add(sc_acc[:, :], sc_acc[:, :], sps[:, 0:256]
                                     ).then_inc(S.VSA, 1)
                for _i in range(NRED_PT):
                    if nred < 2 * NW:
                        c = nred
                        v.wait_ge([S.LWA0, S.LWA1][(c // 2) % 2],
                                  16 * ((c // 2) // 2 + 1))
                        v.reduce_sum(wpart[:, c:c + 1],
                                     wf32[:, (c // 2) % 2, c % 2, :],
                                     axis=AX.X, apply_absolute_value=True
                                     ).then_inc(S.VWA, 1)
                        nred += 1
                        if c == NW - 1:
                            v.reduce_sum(wpart[:, 64:65], wpart[:, 0:NW],
                                         axis=AX.X).then_inc(S.VWA, 1)
                        if c == 2 * NW - 1:
                            v.reduce_sum(wpart[:, 65:66], wpart[:, NW:2 * NW],
                                         axis=AX.X).then_inc(S.VWA, 1)
                if t == T_SSC:
                    v.wait_ge(S.TSM, 1)
                    v.tensor_scalar(ssc[0:1, 0:1], sps[0:1, 300:301],
                                    WMEAN, 1e-5, ALU.mult, ALU.max)
                    v.tensor_scalar(ssc[0:1, 1:2], sps[0:1, 301:302],
                                    WMEAN, 1e-5, ALU.mult, ALU.max
                                    ).then_inc(S.VSC, 1)
                if t == T_ISB:
                    v.wait_ge(S.TSM, 2)
                    v.tensor_copy(isb[:, 2:3], sps[:, 302:303])
                    v.tensor_copy(isb[:, 3:4], sps[:, 303:304])
                    v.tensor_scalar(isb[:, 0:1], isb[:, 2:3], 0.5, None,
                                    ALU.mult)
                    v.tensor_scalar(isb[:, 1:2], isb[:, 3:4], 0.5, None,
                                    ALU.mult)
                    v.tensor_scalar(isb[:, 4:5], isb[:, 0:1], -1.0, None,
                                    ALU.mult)
                    v.tensor_scalar(isb[:, 6:7], isb[:, 2:3], 1.0 / 127.0,
                                    None, ALU.mult)
                    v.tensor_scalar(isb[:, 7:8], isb[:, 3:4], 1.0 / 127.0,
                                    None, ALU.mult)
                    v.tensor_scalar(isb[:, 5:6], isb[:, 1:2], -1.0, None,
                                    ALU.mult).then_inc(S.VSC, 1)
                if t >= T_TERN + 1:
                    for _i in range(NTERN_PT + 1):
                        if (nev < 2 * NW and nev < nge
                                and nev < (t - T_TERN) * NTERN_PT + 1):
                            wt_evac(nev)
                            nev += 1
                if t >= T_TERN:
                    for _i in range(NTERN_PT):
                        if nge < 2 * NW and nge < (t - T_TERN + 1) * NTERN_PT:
                            tern_ge(nge)
                            nge += 1
            v.wait_ge(S.TS, 64)
            v.tensor_add(sc_acc[:, :], sc_acc[:, :], sps[:, 0:256]
                         ).then_inc(S.VSA, 1)
            while nge < 2 * NW or nev < 2 * NW:
                if nev < 2 * NW and nev <= nge - 1:
                    wt_evac(nev)
                    nev += 1
                if nge < 2 * NW:
                    tern_ge(nge)
                    nge += 1
            v.reduce_max(smax[:, 0:4].rearrange("p (g o) -> p g o", o=1),
                         sc_acc[:, :].rearrange("p (g e) -> p g e", g=4),
                         axis=AX.X)
            v.tensor_scalar(smax[:, 4:8], smax[:, 0:4], -0.125, None,
                            ALU.mult).then_inc(S.VSM1, 1)
            v.wait_ge(S.SSM, 4)
            v.reduce_sum(smax[:, 8:12].rearrange("p (g o) -> p g o", o=1),
                         sc_acc[:, :].rearrange("p (g e) -> p g e", g=4),
                         axis=AX.X)
            v.reciprocal(smax[:, 12:16], smax[:, 8:12])
            mm = None
            for cs in range(4):
                mm = v.tensor_scalar(attn_sb[:, cs * 64:(cs + 1) * 64],
                                     sc_acc[:, cs * 64:(cs + 1) * 64],
                                     smax[:, 12 + cs:13 + cs], None, ALU.mult)
            mm.then_inc(S.SM, 1)

        @block.scalar
        def _(sc):
            sc = SW.bind(sc)
            for t in range(NT):
                if t < 16 and t % 4 == 3:
                    w = t // 4
                    sc.wait_ge(S.LS, 16)
                    for j in range(4):
                        sc.wait_ge(S.TVJ, 4 * w + j + 1)
                        sc.activation(vt[:, j, w * 512:(w + 1) * 512],
                                      vps[:, j % 2, :], AF.Identity,
                                      bias=bqv[:, j:j + 1]).then_inc(S.SV, 1)
            sc.wait_ge(S.VSM1, 1)
            for cs in range(4):
                sc.activation(sc_acc[:, cs * 64:(cs + 1) * 64],
                              sc_acc[:, cs * 64:(cs + 1) * 64],
                              AF.Exp, bias=smax[:, 4 + cs:5 + cs],
                              scale=0.125).then_inc(S.SSM, 1)

        actx.close()
        pctx.close()

        bctx = ExitStack()
        Eb = bctx.enter_context
        wfc_sb = Eb(nc.sbuf_tensor("wfc_sb", [128, 4, 1024], BF16))
        bfc_bf = Eb(nc.sbuf_tensor("bfc_bf", [1, D], BF16))
        gamma_r = Eb(nc.sbuf_tensor("gamma_r", [1, D], BF16))
        gamma_b = Eb(nc.sbuf_tensor("gamma_b", [128, 1024], BF16))
        beta_b = Eb(nc.sbuf_tensor("beta_b", [128, 1024], BF16))
        xo = Eb(nc.sbuf_tensor("xo", [128, 2, 1024], BF16))
        ao_sb = Eb(nc.sbuf_tensor("ao_sb", [128, 512], BF16))
        t_a = Eb(nc.sbuf_tensor("t_a", [128, 3, 1024], F32))
        t_b = Eb(nc.sbuf_tensor("t_b", [128, 1024], BF16))
        x1_sb2 = Eb(nc.sbuf_tensor("x1_sb2", [128, 2, 1024], BF16))
        xqT_sb = Eb(nc.sbuf_tensor("xqT_sb", [128, 1024], BF16))
        g_sb = Eb(nc.sbuf_tensor("g_sb", [128, 4096], F32))

        pbctx = ExitStack()
        Epb = pbctx.enter_context
        aops = Epb(nc.psum_tensor("aops", [128, 512], F32))
        fcps = Epb(nc.psum_tensor("fcps", [128, 2, 512], F32))
        xtps = Epb(nc.psum_tensor("xtps", [128, 1024], BF16))
        hps = Epb(nc.psum_tensor("hps", [128, 2, 512], F32))
        o2ps = Epb(nc.psum_tensor("o2ps", [128, 2, 512], F32))

        JOIN = (("TS", 64), ("TVJ", 16), ("TWT", 64), ("VWT", 64),
                ("VSA", 64), ("SM", 1), ("VWA", 66),
                ("SV", 16), ("VWB", 64), ("VQ", 64),
                ("TQ", 64), ("LWB0", 16 * 16), ("LWB1", 16 * 16))

        def join(eng):
            for nm, val in JOIN:
                eng.wait_ge(sems[nm], val)

        @block.gpsimd
        def _(g):
            g = GW.bind(g)
            join(g)
            g.wait_ge(S.LM, 32)
            g.dma_start(out=gamma_r[:, :], in_=gamma[:, :]).then_inc(S.LM, 16)
            g.wait_ge(S.LM, 48)
            g.wait_ge(S.TGB, 1)
            g.dma_start(out=gamma_r[:, :], in_=beta[:, :]).then_inc(S.LM, 16)
            g.wait_ge(S.LM, 64)
            g.dma_start(out=bfc_bf[:, :], in_=b_fc[:, :]).then_inc(S.LM, 16)
            g.wait_ge(S.LM, 80)
            g.dma_start(out=wfc_sb[:, :, :],
                        in_=w_fc.rearrange("(j p) c -> p j c", p=128)
                        ).then_inc(S.LM, 16)
            def g_rounds(j):
                taj = t_a[:, j % 3, :]
                g.wait_ge(S.VCC, j + 1)
                for k in range(8):
                    m, h = k // 2, k % 2
                    if h == 0 and j >= 1 and 4 * j + m - 3 >= 1:
                        g.wait_ge(S.TGT, 4 * j + m - 3)
                    if 8 * j + k - 1 >= 1:
                        g.wait_ge(S.VGQ, 8 * j + k - 1)
                    g.tensor_scalar(
                        taj[:, h * 512:(h + 1) * 512],
                        g_sb[:, m * 1024 + h * 512:m * 1024 + (h + 1) * 512],
                        stat[:, 14:15], MAGIC, ALU.mult, ALU.add
                        ).then_inc(S.GRD, 1)

            for i in range(NO):
                if i >= 2:
                    g.wait_ge(S.VS2, i - 1)
                if i >= 1:
                    g.wait_ge(S.LO, 16 * i)
                g.dma_start(out=xo[:, i % 2, :],
                            in_=x_own[i * 128:(i + 1) * 128, :]
                            ).then_inc(S.LO, 16)
                if i >= 1:
                    g_rounds(i - 1)
            g_rounds(NO - 1)

        @block.tensor
        def _(te):
            join(te)
            te.wait_ge(S.LM, 48)
            mm = None
            for half in range(2):
                mm = te.matmul(fcps[:, half, :], ones_bf[0:1, 0:128],
                               gamma_r[0:1, half * 512:(half + 1) * 512],
                               start=True, stop=True)
            mm.then_inc(S.TGB, 1)
            te.wait_ge(S.VGB, 1)
            te.wait_ge(S.LM, 80)
            mm = None
            for half in range(2):
                mm = te.matmul(fcps[:, half, :], ones_bf[0:1, 0:128],
                               gamma_r[0:1, half * 512:(half + 1) * 512],
                               start=True, stop=True)
            mm.then_inc(S.TGB, 1)
            te.wait_ge(S.LM, 96)

            def emit_ao(i):
                te.wait_ge(S.SM, 1)
                if i >= 1:
                    te.wait_ge(S.VAO, i)
                mm = None
                for h in range(8):
                    pb = (h % 2) * 64
                    mm = te.matmul(
                        aops[pb:pb + 64, (h // 2) * 128:(h // 2) * 128 + 128],
                        attn_sb[pb:pb + 64, (h // 2) * 64:(h // 2) * 64 + 64],
                        vt[pb:pb + 64, h // 2, i * 128:(i + 1) * 128],
                        start=True, stop=True, tile_position=(pb, pb))
                mm.then_inc(S.TAO, 1)

            def emit_fc(i):
                te.wait_ge(S.VAO, i + 1)
                if i == 0:
                    te.wait_ge(S.VGB, 2)
                else:
                    te.wait_ge(S.VR, i)
                mm = None
                for half in range(2):
                    for j in range(4):
                        te.matmul(fcps[:, half, :],
                                  ao_sb[:, j * 128:(j + 1) * 128],
                                  wfc_sb[:, j, half * 512:(half + 1) * 512],
                                  start=(j == 0), stop=False)
                    mm = te.matmul(fcps[:, half, :], ones_bf[0:1, 0:128],
                                   bfc_bf[0:1, half * 512:(half + 1) * 512],
                                   start=False, stop=True)
                mm.then_inc(S.TFC, 1)

            emit_ao(0)
            emit_fc(0)
            for i in range(NO):
                if i + 1 < NO:
                    emit_ao(i + 1)
                te.wait_ge(S.VX1, i + 1)
                if i >= 1:
                    te.wait_ge(S.VGT, 4 * i)
                mm = None
                for kb in range(8):
                    mm = te.transpose(
                        xtps[:, kb * 128:(kb + 1) * 128],
                        t_b[:, kb * 128:(kb + 1) * 128],
                        ident[:])
                mm.then_inc(S.TXQ, 1)
                te.wait_ge(S.VXQ, i + 1)
                if i == 0:
                    te.wait_ge(S.VWT, 32)
                if i + 1 < NO:
                    emit_fc(i + 1)
                for ob in range(8):
                    if 8 * i + ob >= 2:
                        te.wait_ge(S.SG, 8 * i + ob - 1)
                    mm = None
                    for k in range(8):
                        mm = te.matmul(hps[:, ob % 2, :],
                                       xqT_sb[:, k * 128:(k + 1) * 128],
                                       w1T[:, k, ob * 512:(ob + 1) * 512],
                                       start=(k == 0), stop=(k == 7))
                    mm.then_inc(S.TH, 1)
                for b2 in range(4):
                    te.wait_ge(S.VGQ, 8 * i + 2 * b2 + 1)
                    te.wait_ge(S.VGT, 4 * i + b2)
                    mm = None
                    for kb in range(8):
                        if kb == 4:
                            te.wait_ge(S.VGQ, 8 * i + 2 * b2 + 2)
                        blk = b2 * 8 + kb
                        m_ch, kb_in = blk // 8, blk % 8
                        mm = te.transpose(
                            xtps[:, kb * 128:(kb + 1) * 128],
                            g_sb[:, m_ch * 1024 + kb_in * 64:
                                 m_ch * 1024 + (kb_in + 1) * 64].bitcast(BF16),
                            ident[:])
                    mm.then_inc(S.TGT, 1)
                if i == 0:
                    te.wait_ge(S.VWT, 64)
                else:
                    te.wait_ge(S.VRC, i)
                mm = None
                for half in range(2):
                    for k in range(32):
                        b2k, kbk = k // 8, k % 8
                        if kbk == 0:
                            te.wait_ge(S.VGT, 4 * i + b2k + 1)
                        gqT_blk = g_sb[:, b2k * 1024 + 512 + kbk * 64:
                                       b2k * 1024 + 512 + (kbk + 1) * 64
                                       ].bitcast(BF16)
                        mm = te.matmul(o2ps[:, half, :],
                                       gqT_blk,
                                       w2T[:, k, half * 512:(half + 1) * 512],
                                       start=(k == 0), stop=(k == 31))
                mm.then_inc(S.TF2, 1)

        @block.vector
        def _(v):
            v = VW.bind(v)
            join(v)
            v.wait_ge(S.TGB, 1)
            v.tensor_copy(gamma_b[:, :], fcps[:, :, :].rearrange('p a b -> p (a b)')).then_inc(S.VGB, 1)
            v.wait_ge(S.TGB, 2)
            v.tensor_copy(beta_b[:, :], fcps[:, :, :].rearrange('p a b -> p (a b)')).then_inc(S.VGB, 1)
            v.wait_ge(S.TAO, 1)
            v.tensor_copy(ao_sb[:, :], aops[:, :]).then_inc(S.VAO, 1)

            def v_r(j):
                ta = t_a[:, j % 3, :]
                v.wait_ge(S.TFC, j + 1)
                v.wait_ge(S.LO, 16 * (j + 1))
                if j >= 3:
                    v.wait_ge(S.OUT, 16 * (j - 2))
                v.scalar_tensor_tensor(
                    ta, fcps[:, :, :].rearrange('p a b -> p (a b)'),
                    1.0, xo[:, j % 2, :], ALU.mult, ALU.add
                    ).then_inc(S.VR, 1)

            def v_ych(j):
                ta = t_a[:, j % 3, :]
                v.wait_ge(S.SQ1, j + 1)
                emit_lntiny(v, stat, 0, 1, 2, 3, 1e-6).then_inc(S.VS1, 1)
                if j >= 1:
                    v.wait_ge(S.TXQ, j)
                v.tensor_scalar(t_b[:, :], ta, stat[:, 2:3], stat[:, 3:4],
                                ALU.mult, ALU.add)
                v.tensor_mul(t_b[:, :], t_b[:, :], gamma_b[:, :])
                v.tensor_add(t_b[:, :], t_b[:, :], beta_b[:, :])
                v.tensor_add(ta, t_b[:, :], xo[:, j % 2, :]
                             ).then_inc(S.VRB, 1)

            def v_quant(j):
                ta = t_a[:, j % 3, :]
                c11 = 11 if j % 2 == 0 else 23
                v.wait_ge(S.SQ2, j + 1)
                emit_lntiny(v, stat, 4, 5, 6, 7, 1e-5).then_inc(S.VS2, 1)
                v.wait_ge(S.SX1, j + 1)
                v.reduce_max(stat[:, 8:9], x1_sb2[:, j % 2, :], axis=AX.X,
                             apply_absolute_value=True)
                v.tensor_scalar(stat[:, 8:9], stat[:, 8:9], 1e-5, None,
                                ALU.max)
                v.reciprocal(stat[:, 9:10], stat[:, 8:9])
                v.tensor_scalar(stat[:, 10:11], stat[:, 9:10], 127.0, None,
                                ALU.mult)
                v.tensor_mul(stat[:, 36:37], stat[:, 6:7], stat[:, 10:11])
                v.tensor_mul(stat[:, 37:38], stat[:, 7:8], stat[:, 10:11])
                v.tensor_scalar(stat[:, 37:38], stat[:, 37:38], MAGIC, None,
                                ALU.add)
                v.tensor_mul(stat[:, c11:c11 + 1], stat[:, 8:9], isb[:, 6:7])
                v.tensor_scalar(ta, ta, stat[:, 36:37], stat[:, 37:38],
                                ALU.mult, ALU.add)
                v.tensor_scalar(t_b[:, :], ta, MAGIC, None, ALU.subtract
                                ).then_inc(S.VX1, 1)

            def v_xqevac(j):
                v.wait_ge(S.TXQ, j + 1)
                if j >= 1:
                    v.wait_ge(S.TH, 8 * j)
                v.tensor_copy(xqT_sb[:, :], xtps[:, :]).then_inc(S.VXQ, 1)

            def v_aoevac(j):
                v.wait_ge(S.TAO, j + 1)
                v.tensor_copy(ao_sb[:, :], aops[:, :]).then_inc(S.VAO, 1)

            def v_amax(j, ob):
                v.wait_ge(S.SG, 8 * j + ob + 1)
                v.reduce_max(stat[:, 24 + ob:25 + ob],
                             g_sb[:, ob * 512:(ob + 1) * 512], axis=AX.X,
                             apply_absolute_value=True)

            def v_sc2(j):
                v.reduce_max(stat[:, 12:13], stat[:, 24:32], axis=AX.X,
                             apply_absolute_value=True)
                v.tensor_scalar(stat[:, 12:13], stat[:, 12:13], 1e-5, None,
                                ALU.max)
                v.tensor_scalar(stat[:, 13:14], stat[:, 12:13], 1.0 / 127.0,
                                None, ALU.mult)
                v.reciprocal(stat[:, 14:15], stat[:, 13:14]).then_inc(S.VCC, 1)
                v.tensor_mul(stat[:, 15:16], stat[:, 12:13], isb[:, 7:8])

            def v_evac(j, b2):
                v.wait_ge(S.TGT, 4 * j + b2 + 1)
                if j >= 1:
                    v.wait_ge(S.TF2, j)
                v.tensor_copy(
                    g_sb[:, b2 * 1024 + 512:(b2 + 1) * 1024].bitcast(BF16),
                    xtps[:, :]).then_inc(S.VGT, 1)

            def v_r3(j):
                ta = t_a[:, j % 3, :]
                v.wait_ge(S.TF2, j + 1)
                v.scalar_tensor_tensor(
                    ta, o2ps[:, :, :].rearrange('p a b -> p (a b)'),
                    stat[:, 15:16], x1_sb2[:, j % 2, :], ALU.mult, ALU.add
                    ).then_inc(S.VRC, 1)

            def v_lnt3(j):
                v.wait_ge(S.SQ3, j + 1)
                emit_lntiny(v, stat, 16, 17, 18, 19, 1e-5, sum_c=39,
                            sq_c=38).then_inc(S.VS3, 1)

            def v_out(j):
                ta = t_a[:, j % 3, :]
                v.wait_ge(S.VS3, j + 1)
                if j >= 1:
                    v.wait_ge(S.OUT, 16 * j)
                v.tensor_scalar(ta, ta, stat[:, 18:19], stat[:, 19:20],
                                ALU.mult, ALU.add).then_inc(S.SO, 1)

            v_r(0)
            v_ych(0)
            v_quant(0)
            for i in range(NO):
                v_xqevac(i)
                if i + 1 < NO:
                    v_aoevac(i + 1)
                if i >= 1:
                    v_r3(i - 1)
                if i + 1 < NO:
                    v_r(i + 1)
                if i >= 1:
                    v_lnt3(i - 1)
                    v_out(i - 1)
                v_amax(i, 0)
                v_amax(i, 1)
                if i + 1 < NO:
                    v_ych(i + 1)
                v_amax(i, 2)
                v_amax(i, 3)
                for ob in range(4, 8):
                    v_amax(i, ob)
                v_sc2(i)
                for b2 in range(4):
                    v_evac(i, b2)
                if i + 1 < NO:
                    v_quant(i + 1)
            v_r3(NO - 1)
            v_lnt3(NO - 1)
            v_out(NO - 1)

        @block.scalar
        def _(sc):
            sc = SW.bind(sc)
            join(sc)
            def s_gelu(i, ob):
                c11 = 11 if i % 2 == 0 else 23
                sc.wait_ge(S.TH, 8 * i + ob + 1)
                sc.activation(g_sb[:, ob * 512:(ob + 1) * 512],
                              hps[:, ob % 2, :], AF.Gelu,
                              scale=stat[:, c11:c11 + 1]).then_inc(S.SG, 1)

            def s_ln1(i):
                ta = t_a[:, i % 3, :]
                sc.wait_ge(S.VR, i + 1)
                if i >= 1:
                    sc.wait_ge(S.TXQ, i)
                sc.activation(t_b[:, :], ta, AF.Square,
                              accum_out=stat[:, 21:22])
                sc.activation(ta, ta, AF.Identity,
                              accum_out=stat[:, 20:21]).then_inc(S.SQ1, 1)

            def s_ln2(i):
                ta = t_a[:, i % 3, :]
                sc.wait_ge(S.VRB, i + 1)
                sc.activation(t_b[:, :], ta, AF.Square,
                              accum_out=stat[:, 21:22])
                sc.activation(ta, ta, AF.Identity,
                              accum_out=stat[:, 20:21]).then_inc(S.SQ2, 1)

            def s_x1(i):
                ta = t_a[:, i % 3, :]
                sc.wait_ge(S.VS2, i + 1)
                if i >= 2:
                    sc.wait_ge(S.VRC, i - 1)
                sc.activation(x1_sb2[:, i % 2, :], ta, AF.Identity,
                              bias=stat[:, 7:8], scale=stat[:, 6:7]
                              ).then_inc(S.SX1, 1)

            def s_ln3(i):
                ta = t_a[:, i % 3, :]
                sc.wait_ge(S.VRC, i + 1)
                sc.activation(x1_sb2[:, i % 2, :], ta, AF.Square,
                              accum_out=stat[:, 38:39])
                sc.activation(ta, ta, AF.Identity,
                              accum_out=stat[:, 39:40]).then_inc(S.SQ3, 1)

            def s_sub(i, k):
                m, h = k // 2, k % 2
                sc.wait_ge(S.GRD, 8 * i + k + 1)
                sc.activation(
                    g_sb[:, m * 1024 + h * 256:m * 1024 + (h + 1) * 256
                         ].bitcast(BF16),
                    t_a[:, i % 3, h * 512:(h + 1) * 512], AF.Identity,
                    bias=cmag[:, 1:2]).then_inc(S.VGQ, 1)

            s_ln1(0)
            s_ln2(0)
            s_x1(0)
            for i in range(NO):
                if i >= 1:
                    s_ln3(i - 1)
                s_gelu(i, 0)
                s_gelu(i, 1)
                if i + 1 < NO:
                    s_ln1(i + 1)
                for ob in range(2, 8):
                    s_gelu(i, ob)
                if i + 1 < NO:
                    s_ln2(i + 1)
                for k in range(8):
                    s_sub(i, k)
                if i + 1 < NO:
                    s_x1(i + 1)
            s_ln3(NO - 1)

        @block.sync
        def _(s):
            for i in range(NO):
                s.wait_ge(S.SO, i + 1)
                s.dma_start(out=y[i * 128:(i + 1) * 128, :],
                            in_=t_a[:, i % 3, :]).then_inc(S.OUT, 16)
            s.wait_ge(S.OUT, 16 * NO)

        bctx.close()
        pbctx.close()

    return nc


_programs = {}


def kernel(L0_attn_x, L0_attn_w_qkv, L0_attn_b_qkv, L0_attn_w_fc,
           L0_attn_b_fc, L0_attn_gamma, L0_attn_beta, L0_ffn_w1, L0_ffn_w2):
    x = np.ascontiguousarray(np.asarray(L0_attn_x, dtype=np.float32))
    w_qkv = np.ascontiguousarray(np.asarray(L0_attn_w_qkv, dtype=np.float32))
    b_qkv = np.asarray(L0_attn_b_qkv, dtype=np.float32).reshape(1, 1536)
    w_fc = np.ascontiguousarray(np.asarray(L0_attn_w_fc, dtype=np.float32))
    b_fc = np.asarray(L0_attn_b_fc, dtype=np.float32).reshape(1, D)
    gamma = np.asarray(L0_attn_gamma, dtype=np.float32).reshape(1, D)
    beta = np.asarray(L0_attn_beta, dtype=np.float32).reshape(1, D)
    w1 = np.ascontiguousarray(np.asarray(L0_ffn_w1, dtype=np.float32))
    w2 = np.ascontiguousarray(np.asarray(L0_ffn_w2, dtype=np.float32))
    bqv = np.ascontiguousarray(b_qkv[0, 1024:1536].reshape(4, 128).T)

    nc = _programs.get("nc")
    if nc is None:
        nc = build_program()
        _programs["nc"] = nc

    in_maps = []
    for c in range(8):
        b, qt = c // 4, c % 4
        xb = x[b]
        rolled = np.concatenate([xb[qt * 2048:(qt + 1) * 2048],
                                 xb[:qt * 2048],
                                 xb[(qt + 1) * 2048:]], axis=0)
        in_maps.append({
            "xTt": np.ascontiguousarray(
                rolled.reshape(64, 128, 8, 128).transpose(0, 3, 2, 1)
                .reshape(NT * 128, D)),
            "x_own": np.ascontiguousarray(xb[qt * 2048:(qt + 1) * 2048]),
            "w_qkv": w_qkv, "b_qkv": b_qkv, "bqv_in": bqv,
            "w_fc": w_fc, "b_fc": b_fc, "gamma": gamma, "beta": beta,
            "w1": w1, "w2": w2,
        })

    res = run_bass_kernel_spmd(nc, in_maps, core_ids=list(range(8)))
    out = np.empty((2, 8192, D), dtype=np.float32)
    for c in range(8):
        b, qt = c // 4, c % 4
        out[b, qt * 2048:(qt + 1) * 2048] = np.asarray(res.results[c]["y"])
    return out
